# version 27
# baseline (speedup 1.0000x reference)
import numpy as np
from contextlib import ExitStack
import concourse.bass as bass
import concourse.mybir as mybir
from concourse.bass import IndirectOffsetOnAxis
from concourse.bass_utils import run_bass_kernel_spmd

F32 = mybir.dt.float32
I32 = mybir.dt.int32
BF16 = mybir.dt.bfloat16
AF = mybir.ActivationFunctionType
ALU = mybir.AluOpType
AX = mybir.AxisListType
f32 = np.float32

L, D, U3, FFD = 2048, 1024, 3072, 4096
NT = L // 128
TC, NCH = 256, 8
TF, NTF = 512, 4
NQ = 4
GEL_A = f32(0.2114593)
GEL_B = f32(0.7978845608)
EPS_LN = f32(1e-5)
EPS_DEN = f32(1e-6)
BISECT_ITERS = 10
BIG = 1.0e5


def build_program(debug=False, stop_after="full"):
    nc = bass.Bass("TRN2", target_bir_lowering=False, debug=False)

    x_hbm = nc.dram_tensor("x", [L, D], F32, kind="ExternalInput").ap()
    Wrkv_hbm = nc.dram_tensor("W_rkv", [D, U3], F32, kind="ExternalInput").ap()
    brkv_hbm = nc.dram_tensor("b_rkv", [U3], F32, kind="ExternalInput").ap()
    Wo_hbm = nc.dram_tensor("W_o", [D, D], F32, kind="ExternalInput").ap()
    bo_hbm = nc.dram_tensor("b_o", [D], F32, kind="ExternalInput").ap()
    nu_hbm = nc.dram_tensor("nu_log", [D], F32, kind="ExternalInput").ap()
    gl_hbm = nc.dram_tensor("gamma_log", [D], F32, kind="ExternalInput").ap()
    wr_hbm = nc.dram_tensor("w_router", [D], F32, kind="ExternalInput").ap()
    W1_hbm = nc.dram_tensor("W1", [D, FFD], F32, kind="ExternalInput").ap()
    b1_hbm = nc.dram_tensor("b1", [FFD], F32, kind="ExternalInput").ap()
    W2_hbm = nc.dram_tensor("W2", [FFD, D], F32, kind="ExternalInput").ap()
    b2_hbm = nc.dram_tensor("b2", [D], F32, kind="ExternalInput").ap()
    y_hbm = nc.dram_tensor("y", [L, D], F32, kind="ExternalOutput").ap()
    xn_dram = nc.dram_tensor("xn_dram", [L, D + 4], BF16, kind="Internal").ap()
    xsel_dram = nc.dram_tensor("xsel_dram", [L // 2, D + 4], BF16, kind="Internal").ap()
    pairs_dram = nc.dram_tensor("pairs_dram", [L // 2, 2], F32, kind="Internal").ap()
    upd_dram = nc.dram_tensor("upd_dram", [L, D], F32, kind="Internal").ap()
    wbf = nc.dram_tensor("wbf", [12 * 1024 * 1024], BF16, kind="Internal").ap()
    zdram = nc.dram_tensor("zdram", [128, D], F32, kind="Internal").ap()
    W1_bf = wbf[0:D * FFD].rearrange("(a b) -> a b", b=FFD)
    W2_bf = wbf[D * FFD:2 * D * FFD].rearrange("(a b) -> a b", b=D)
    Wrkv_bf = wbf[2 * D * FFD:2 * D * FFD + D * U3].rearrange("(a b) -> a b", b=U3)
    Wo_bf = wbf[2 * D * FFD + D * U3:2 * D * FFD + D * U3 + D * D].rearrange(
        "(a b) -> a b", b=D)
    dbg = {}
    if debug:
        for nm, shp in [("d_scores", [128, NT]), ("d_thr", [128, 4]),
                        ("d_wsel", [128, NT]), ("d_acc", [128, NT * D])]:
            dbg[nm] = nc.dram_tensor(nm, shp, F32, kind="ExternalOutput").ap()

    plan = {"sync": [], "gpsimd": [], "tensor": [], "scalar": [], "vector": []}
    cnt = {"sp": 0, "sa": 0, "sv": 0, "sg": 0}
    for _q in ("sx", "sw", "so", "sz"):
        for _i in range(8):
            cnt[f"{_q}{_i}"] = 0
    sems = {}

    dead = [False]

    def op(eng, fn):
        if not dead[0]:
            plan[eng].append(fn)

    def wt(eng, sem, val):
        assert 0 <= val <= cnt[sem], (sem, val, cnt[sem])
        if val > 0:
            op(eng, lambda e, s=sem, v=val: e.wait_ge(sems[s], v))

    def inc(sem, n=1):
        cnt[sem] += n
        return cnt[sem]

    _OWN = {"vector": "sv", "scalar": "sa", "gpsimd": "sg"}

    def run(eng, sem, fn):
        own = _OWN.get(eng)
        if own is not None and cnt[own] > 0:
            wt(eng, own, cnt[own])
        inc(sem)
        def wrapped(e, fn=fn, sem=sem):
            fn(e).then_inc(sems[sem], 1)
        op(eng, wrapped)

    RINGS = {"sx": 8, "sw": 8, "so": 4, "sz": 4}
    _ring_idx = {"sx": 0, "sw": 0, "so": 0, "sz": 0}

    def dma(eng, queue, dst, src):
        slot = _ring_idx[queue] % RINGS[queue]
        _ring_idx[queue] += 1
        sem = f"{queue}{slot}"
        if cnt[sem] > 0:
            wt(eng, sem, cnt[sem])
        inc(sem, 16)
        tok = (sem, cnt[sem])
        def wrapped(e, dst=dst, src=src, sem=sem):
            e.dma_start(out=dst, in_=src).then_inc(sems[sem], 16)
        op(eng, wrapped)
        return tok

    _breg = {}

    def dma_indirect_out(eng, queue, out_t, off_ap, in_ap, bound, cce=None):
        if bound not in _breg:
            _breg[bound] = None
            def init(e, bound=bound):
                r = e.alloc_register(f"bc{bound}")
                e.reg_mov(r, bound)
                _breg[bound] = r
            op(eng, init)
        slot = _ring_idx[queue] % RINGS[queue]
        _ring_idx[queue] += 1
        sem = f"{queue}{slot}"
        if cnt[sem] > 0:
            wt(eng, sem, cnt[sem])
        inc(sem, 16)
        tok = (sem, cnt[sem])
        def wrapped(e, out_t=out_t, off_ap=off_ap, in_ap=in_ap, sem=sem,
                    bound=bound, cce=cce):
            kw = {}
            if cce is not None:
                kw["compute_op"] = cce
            e.indirect_dma_start(
                out=out_t, out_offset=IndirectOffsetOnAxis(ap=off_ap, axis=0),
                in_=in_ap, in_offset=None, bounds_check=_breg[bound],
                oob_is_err=False, **kw).then_inc(sems[sem], 16)
        op(eng, wrapped)
        return tok

    def wtt(eng, toks):
        best = {}
        for s, v in toks:
            best[s] = max(best.get(s, 0), v)
        for s, v in sorted(best.items()):
            wt(eng, s, v)

    with ExitStack() as _ctx:
        def _e(x):
            return _ctx.enter_context(x)
        acc = _e(nc.sbuf_tensor("acc", [128, NT * D], F32))
        xnT = _e(nc.sbuf_tensor("xnT", [128, 8 * L], BF16))
        warena = _e(nc.sbuf_tensor("warena", [128, 32768], BF16))
        tarena = _e(nc.sbuf_tensor("tarena", [128, 8192], F32))
        xnbf = _e(nc.sbuf_tensor("xnbf", [128, 2 * D], BF16))
        bcB = _e(nc.sbuf_tensor("bcB", [128, D], F32))
        pcols = _e(nc.sbuf_tensor("pcols", [128, 96], F32))
        dcols = _e(nc.sbuf_tensor("dcols", [128, 48], F32))
        stat = _e(nc.sbuf_tensor("stat", [128, 8 * NT], F32))
        bis = _e(nc.sbuf_tensor("bis", [128, 64], F32))
        ramp16 = _e(nc.sbuf_tensor("ramp16", [128, 16], F32))
        csum16 = _e(nc.sbuf_tensor("csum16", [128, 16], F32))
        tcols = _e(nc.sbuf_tensor("tcols", [128, 16], F32))
        offsI = _e(nc.sbuf_tensor("offsI", [128, 16], I32))
        toffI = _e(nc.sbuf_tensor("toffI", [128, 8], I32))
        wt_sl = _e(nc.sbuf_tensor("wt_sl", [128, 16], F32))
        pairs_all = _e(nc.sbuf_tensor("pairs_all", [128, 32], F32))
        ibf = _e(nc.sbuf_tensor("ibf", [128, 16], BF16))
        pbf = _e(nc.sbuf_tensor("pbf", [128, 16], BF16))
        tri_f = _e(nc.sbuf_tensor("tri_f", [128, 128], F32))
        qiota = _e(nc.sbuf_tensor("qiota", [128, 128], F32))
        carry = _e(nc.sbuf_tensor("carry", [128, 16], F32))
        ident_b = _e(nc.sbuf_tensor("ident_b", [128, 128], BF16))
        ident_f = _e(nc.sbuf_tensor("ident_f", [128, 128], F32))
        ones_f = _e(nc.sbuf_tensor("ones_f", [1, 128], F32))
        ones128 = _e(nc.sbuf_tensor("ones128", [128, 128], F32))
        praw = _e(nc.sbuf_tensor("praw", [64, 128], F32))
        pb0 = _e(nc.psum_tensor("pb0", [128, 512], F32))
        pb1 = _e(nc.psum_tensor("pb1", [128, 512], F32))
        pb2 = _e(nc.psum_tensor("pb2", [128, 512], F32))
        pb3 = _e(nc.psum_tensor("pb3", [128, 512], F32))
        pb4 = _e(nc.psum_tensor("pb4", [128, 512], F32))
        pb5 = _e(nc.psum_tensor("pb5", [128, 512], F32))
        pb6 = _e(nc.psum_tensor("pb6", [128, 512], F32))
        pb7 = _e(nc.psum_tensor("pb7", [128, 512], F32))
        for _q in ("sx", "sw", "so", "sz"):
            for _i in range(8):
                _nm = f"{_q}{_i}"
                sems[_nm] = _e(nc.semaphore(_nm))
        for _nm in ("sp", "sa", "sv", "sg"):
            sems[_nm] = _e(nc.semaphore(_nm))
        block = _e(nc.Block())
        pb = [pb0, pb1, pb2, pb3, pb4, pb5, pb6, pb7]
        pb7_bf = pb7[:].bitcast(BF16)
        pb6_bf = pb6[:].bitcast(BF16)

        hbuf = tarena[:, 0:4096]
        gbufh = tarena[:, 4096:6144]
        hbf16 = tarena[:, 6144:8192].bitcast(BF16)
        kq = tarena[:, 0:1024]
        vq = tarena[:, 1024:2048]
        srq = tarena[:, 2048:3072]
        y1q = tarena[:, 3072:4096]
        y2q = tarena[:, 4096:5120]
        lamt = tarena[:, 5120:7168]
        xatt = tarena[:, 7168:8192].bitcast(BF16)
        dumpA = tarena[:, 0:1024]
        dumpB = tarena[:, 1024:2048]
        xb = tarena[:, 2048:4096]
        prow1 = tarena[0:1, 4096:5120]
        geD = bis[:, 32:48]
        sigT = bis[:, 48:64]

        S_SUM, S_SQ, S_MEAN, S_VAR, S_RSTD, S_NMR, S_R1P, S_XW = (
            0, 16, 32, 48, 64, 80, 96, 112)
        score_cols = stat[:, S_XW:S_XW + NT]
        wsel = bis[:, 16:32]

        AW1, AW2, AWRKV, AWO = 0, 8192, 0, 24576


        def ckpt(name):
            if stop_after == name:
                emit_final()
                dead[0] = True

        def emit_final():
            for s in ("sv", "sa", "sp", "sg"):
                wt("sync", s, cnt[s])
            toks = []
            for c in range(NCH):
                toks.append(dma(
                    "sync", "so", y_hbm[c * TC:(c + 1) * TC, :].rearrange(
                        "(i p) f -> p i f", p=128),
                    acc[:, c * 2 * D:(c + 1) * 2 * D]
                    .rearrange("p (i f) -> p i f", f=D)))
            wtt("sync", toks)
        sw_small1 = [
            dma("gpsimd", "sw", praw[0:24, :], brkv_hbm.rearrange("(a b) -> a b", b=128)),
            dma("gpsimd", "sw", praw[32:40, :], nu_hbm.rearrange("(a b) -> a b", b=128))]
        sw_wr = [dma("gpsimd", "sw", prow1[:, :], wr_hbm.unsqueeze(0))]
        run("gpsimd", "sg", lambda g: g.memset(ident_b[:], 1.0))
        run("gpsimd", "sg", lambda g: g.affine_select(
            ident_b[:], ident_b[:], [[-1, 128]], ALU.is_equal, 0.0,
            base=0, channel_multiplier=1))
        run("gpsimd", "sg", lambda g: g.memset(ident_f[:], 1.0))
        run("gpsimd", "sg", lambda g: g.affine_select(
            ident_f[:], ident_f[:], [[-1, 128]], ALU.is_equal, 0.0,
            base=0, channel_multiplier=1))
        run("gpsimd", "sg", lambda g: g.memset(ones_f[:], 1.0))
        run("gpsimd", "sg", lambda g: g.memset(ones128[:], 1.0))
        run("gpsimd", "sg", lambda g: g.memset(carry[:], 0.0))
        run("gpsimd", "sg", lambda g: g.iota(
            ramp16[:], [[1, 16]], base=1, channel_multiplier=0,
            allow_small_or_imprecise_dtypes=True))
        run("gpsimd", "sg", lambda g: g.iota(
            tcols[:], [[128, 16]], base=0, channel_multiplier=1,
            allow_small_or_imprecise_dtypes=True))
        run("gpsimd", "sg", lambda g: g.iota(
            ibf[:], [[1, 16]], base=0, channel_multiplier=0,
            allow_small_or_imprecise_dtypes=True))
        run("gpsimd", "sg", lambda g: g.iota(
            pbf[:], [[0, 16]], base=0, channel_multiplier=1,
            allow_small_or_imprecise_dtypes=True))
        run("gpsimd", "sg", lambda g: g.iota(
            tri_f[:], [[0, 128]], base=0, channel_multiplier=1,
            allow_small_or_imprecise_dtypes=True))
        run("gpsimd", "sg", lambda g: g.iota(
            qiota[:], [[1, 128]], base=0, channel_multiplier=0,
            allow_small_or_imprecise_dtypes=True))
        wt("vector", "sg", cnt["sg"])
        run("vector", "sv", lambda v: v.tensor_tensor(
            out=tri_f[:], in0=tri_f[:], in1=qiota[:], op=ALU.is_lt))
        run("gpsimd", "sg", lambda g: g.memset(tarena[:, 7168:8192], 0.0))
        sg_setup = cnt["sg"]
        wtt("tensor", sw_small1)
        wt("tensor", "sg", sg_setup)
        run("tensor", "sp", lambda t: t.matmul(
            pb5[:, 0:24], praw[0:24, :], ident_f[0:24, 0:24],
            start=True, stop=True, is_transpose=True))
        run("tensor", "sp", lambda t: t.matmul(
            pb6[:, 0:8], praw[32:40, :], ident_f[32:40, 32:40],
            start=True, stop=True, is_transpose=True))
        sp_tr1 = cnt["sp"]
        wt("scalar", "sp", sp_tr1)
        run("scalar", "sa", lambda s: s.activation(
            out=pcols[:, 0:24], in_=pb5[:, 0:24], func=AF.Identity))
        run("scalar", "sa", lambda s: s.activation(
            out=pcols[:, 24:32], in_=pb6[:, 0:8], func=AF.Identity))
        sa_tr1 = cnt["sa"]
        wt("gpsimd", "sp", sp_tr1)
        sw_small2 = [
            dma("gpsimd", "sw", praw[0:8, :], gl_hbm.rearrange("(a b) -> a b", b=128)),
            dma("gpsimd", "sw", praw[32:64, :],
                b1_hbm.rearrange("(a b) -> a b", b=128))]
        wtt("tensor", sw_small2)
        wt("tensor", "sa", sa_tr1)
        run("tensor", "sp", lambda t: t.matmul(
            pb5[:, 0:8], praw[0:8, :], ident_f[0:8, 0:8],
            start=True, stop=True, is_transpose=True))
        run("tensor", "sp", lambda t: t.matmul(
            pb6[:, 0:32], praw[32:64, :], ident_f[32:64, 32:64],
            start=True, stop=True, is_transpose=True))
        sp_tr2 = cnt["sp"]
        wt("scalar", "sp", sp_tr2)
        run("scalar", "sa", lambda s: s.activation(
            out=pcols[:, 32:40], in_=pb5[:, 0:8], func=AF.Identity))
        run("scalar", "sa", lambda s: s.activation(
            out=pcols[:, 40:72], in_=pb6[:, 0:32], func=AF.Identity))
        sa_tr2 = cnt["sa"]

        wtt("tensor", sw_wr)
        wt("tensor", "sg", sg_setup)
        run("tensor", "sp", lambda t: t.matmul(
            pb3[:, 0:512], ones_f[:, :], prow1[:, 0:512], start=True, stop=True))
        run("tensor", "sp", lambda t: t.matmul(
            pb4[:, 0:512], ones_f[:, :], prow1[:, 512:1024], start=True, stop=True))
        sp_wrB = cnt["sp"]
        wt("scalar", "sp", sp_wrB)
        run("scalar", "sa", lambda s: s.activation(
            out=bcB[:, 0:512], in_=pb3[:, 0:512], func=AF.Identity))
        run("scalar", "sa", lambda s: s.activation(
            out=bcB[:, 512:1024], in_=pb4[:, 0:512], func=AF.Identity))
        sa_wrB = cnt["sa"]

        wt("vector", "sa", sa_tr1)
        run("vector", "sv", lambda v: v.tensor_single_scalar(
            out=dcols[:, 0:8], in_=pcols[:, 0:8], scalar=-1.0, op=ALU.mult))
        wt("scalar", "sv", cnt["sv"])
        run("scalar", "sa", lambda s: s.activation(
            out=dcols[:, 25:33], in_=pcols[:, 24:32], func=AF.Exp))
        run("scalar", "sa", lambda s: s.activation(
            out=dcols[:, 8:16], in_=dcols[:, 25:33], func=AF.Exp,
            scale=f32(-1.0)))
        sa_lam = cnt["sa"]
        wt("vector", "sa", sa_lam)
        run("vector", "sv", lambda v: v.tensor_add(
            out=dcols[:, 25:33], in0=pcols[:, 24:32], in1=pcols[:, 32:40]))
        wt("scalar", "sv", cnt["sv"])
        run("scalar", "sa", lambda s: s.activation(
            out=dcols[:, 16:24], in_=dcols[:, 25:33], func=AF.Exp))
        wt("vector", "sa", cnt["sa"])
        run("vector", "sv", lambda v: v.tensor_single_scalar(
            out=dcols[:, 16:24], in_=dcols[:, 16:24], scalar=1.0,
            op=ALU.subtract))
        sv_gam = cnt["sv"]

        ckpt("setup")
        wt("gpsimd", "sp", sp_wrB)
        sw_bo = [dma("gpsimd", "sw", prow1[:, :], bo_hbm.unsqueeze(0))]
        W1B = {0: 8192, 1: 24576, 2: 16384, 3: 8192}
        W2B = {0: 16384, 1: 8192, 2: 24576, 3: 16384}
        cast_rkv = {0: [], 1: []}
        for hh in (0, 1):
            for s3 in range(3):
                c0 = s3 * 1024 + hh * 512
                cast_rkv[hh].append(dma(
                    "gpsimd", "sw", Wrkv_bf[:, c0:c0 + 512],
                    Wrkv_hbm[:, c0:c0 + 512]))
        cast_wo = [dma("gpsimd", "sw", Wo_bf, Wo_hbm)]
        cast_w12 = [dma("gpsimd", "sw", W1_bf, W1_hbm),
                    dma("gpsimd", "sw", W2_bf, W2_hbm)]

        def w1q_srcb(q):
            return W1_bf[:, q * 1024:(q + 1) * 1024].rearrange(
                "(fi p) F -> p fi F", p=128)

        def w2q_srcb(q):
            return W2_bf[q * 1024:(q + 1) * 1024, :].rearrange(
                "(fi p) F -> p fi F", p=128)

        AWO2, AWRKV2 = 24576, 0

        sa_sweep1_h = {}
        x_toks = []
        for i in range(NT):
            accv = acc[:, i * D:(i + 1) * D]
            tok_x = dma("sync", "sx", accv, x_hbm[i * 128:(i + 1) * 128, :])
            x_toks.append(tok_x)
            wtt("scalar", [tok_x])
            run("scalar", "sa", lambda s, accv=accv, i=i: s.activation(
                out=dumpA, in_=accv, func=AF.Identity,
                accum_out=stat[:, S_SUM + i:S_SUM + i + 1]))
            if i in (7, 15):
                sa_sweep1_h[i // 8] = cnt["sa"]
            wtt("vector", [tok_x])
            if i == 0:
                wt("vector", "sa", sa_wrB)
            run("vector", "sv", lambda v, accv=accv, i=i: v.scalar_tensor_tensor(
                out=dumpB, in0=accv, scalar=1.0,
                in1=accv, op0=ALU.mult, op1=ALU.mult,
                accum_out=stat[:, S_SQ + i:S_SQ + i + 1]))
            run("vector", "sv", lambda v, accv=accv, i=i: v.scalar_tensor_tensor(
                out=dumpB, in0=accv, scalar=1.0,
                in1=bcB[:], op0=ALU.mult, op1=ALU.mult,
                accum_out=stat[:, S_XW + i:S_XW + i + 1]))
        sa_sweep1, sv_sweep1 = cnt["sa"], cnt["sv"]
        sw_att_h = {}
        for hh in (0, 1):
            wtt("sync", cast_rkv[hh])
            sw_att_h[hh] = [dma(
                "sync", "sx",
                warena[:, AWRKV2:AWRKV2 + 8 * U3].rearrange(
                    "p (fi u) -> p fi u", u=U3)[
                    :, :, s3 * 1024 + hh * 512:s3 * 1024 + hh * 512 + 512],
                Wrkv_bf.rearrange("(fi p) u -> p fi u", p=128)[
                    :, :, s3 * 1024 + hh * 512:s3 * 1024 + hh * 512 + 512])
                for s3 in range(3)]
        wtt("sync", cast_wo)
        sw_att_wo = [dma(
            "sync", "sx",
            warena[:, AWO2:AWO2 + 8 * D].rearrange("p (ui d) -> p ui d", d=D),
            Wo_bf.rearrange("(ui p) d -> p ui d", p=128))]

        ckpt("sweep1")
        sv_colmath_h = {}
        for h in (0, 1):
            cl, ch_ = h * 8, h * 8 + 8
            wt("vector", "sa", sa_sweep1_h[h])
            run("vector", "sv", lambda v, cl=cl, ch_=ch_: v.tensor_single_scalar(
                out=stat[:, S_MEAN + cl:S_MEAN + ch_],
                in_=stat[:, S_SUM + cl:S_SUM + ch_],
                scalar=1.0 / D, op=ALU.mult))
            run("vector", "sv", lambda v, cl=cl, ch_=ch_: v.tensor_single_scalar(
                out=stat[:, S_VAR + cl:S_VAR + ch_],
                in_=stat[:, S_SQ + cl:S_SQ + ch_],
                scalar=1.0 / D, op=ALU.mult))
            run("vector", "sv", lambda v, cl=cl, ch_=ch_: v.tensor_mul(
                out=stat[:, S_SUM + cl:S_SUM + ch_],
                in0=stat[:, S_MEAN + cl:S_MEAN + ch_],
                in1=stat[:, S_MEAN + cl:S_MEAN + ch_]))
            run("vector", "sv", lambda v, cl=cl, ch_=ch_: v.tensor_sub(
                out=stat[:, S_VAR + cl:S_VAR + ch_],
                in0=stat[:, S_VAR + cl:S_VAR + ch_],
                in1=stat[:, S_SUM + cl:S_SUM + ch_]))
            sv_var = cnt["sv"]
            wt("scalar", "sv", sv_var)
            run("scalar", "sa", lambda s, cl=cl, ch_=ch_: s.activation(
                out=stat[:, S_RSTD + cl:S_RSTD + ch_],
                in_=stat[:, S_VAR + cl:S_VAR + ch_],
                func=AF.Ln, bias=EPS_LN))
            run("scalar", "sa", lambda s, cl=cl, ch_=ch_: s.activation(
                out=stat[:, S_RSTD + cl:S_RSTD + ch_],
                in_=stat[:, S_RSTD + cl:S_RSTD + ch_],
                func=AF.Exp, scale=f32(-0.5)))
            sa_rstd = cnt["sa"]
            wt("vector", "sa", sa_rstd)
            run("vector", "sv", lambda v, cl=cl, ch_=ch_: v.scalar_tensor_tensor(
                out=stat[:, S_NMR + cl:S_NMR + ch_],
                in0=stat[:, S_MEAN + cl:S_MEAN + ch_],
                scalar=-1.0, in1=stat[:, S_RSTD + cl:S_RSTD + ch_],
                op0=ALU.mult, op1=ALU.mult))
            run("vector", "sv", lambda v, cl=cl, ch_=ch_: v.tensor_single_scalar(
                out=stat[:, S_R1P + cl:S_R1P + ch_],
                in_=stat[:, S_RSTD + cl:S_RSTD + ch_],
                scalar=1.0, op=ALU.add))
            if h == 0:
                run("vector", "sv", lambda v: v.tensor_reduce(
                    out=dcols[:, 24:25], in_=bcB[:], axis=AX.X, op=ALU.add))
            run("vector", "sv", lambda v, cl=cl, ch_=ch_: v.tensor_mul(
                out=stat[:, S_XW + cl:S_XW + ch_],
                in0=stat[:, S_XW + cl:S_XW + ch_],
                in1=stat[:, S_RSTD + cl:S_RSTD + ch_]))
            run("vector", "sv", lambda v, cl=cl, ch_=ch_: v.scalar_tensor_tensor(
                out=score_cols[:, cl:ch_], in0=stat[:, S_NMR + cl:S_NMR + ch_],
                scalar=dcols[:, 24:25], in1=stat[:, S_XW + cl:S_XW + ch_],
                op0=ALU.mult, op1=ALU.add))
            sv_colmath_h[h] = cnt["sv"]
        sv_colmath = cnt["sv"]

        ckpt("colmath")
        wtt("tensor", sw_bo)
        wt("tensor", "sa", sa_wrB)
        run("tensor", "sp", lambda t: t.matmul(
            pb3[:, 0:512], ones_f[:, :], prow1[:, 0:512], start=True, stop=True))
        run("tensor", "sp", lambda t: t.matmul(
            pb4[:, 0:512], ones_f[:, :], prow1[:, 512:1024], start=True, stop=True))
        sp_boB = cnt["sp"]
        wt("scalar", "sp", sp_boB)
        wt("scalar", "sv", sv_colmath)
        run("scalar", "sa", lambda s: s.activation(
            out=bcB[:, 0:512], in_=pb3[:, 0:512], func=AF.Identity))
        run("scalar", "sa", lambda s: s.activation(
            out=bcB[:, 512:1024], in_=pb4[:, 0:512], func=AF.Identity))
        sa_boB = cnt["sa"]

        run("vector", "sv", lambda v: v.memset(bis[:, 0:8], 0.0))
        run("vector", "sv", lambda v: v.memset(bis[:, 0:1], -BIG))
        run("vector", "sv", lambda v: v.memset(bis[:, 2:3], 2.0 * BIG / 16))
        sa_bis_box = [cnt["sa"]]

        pb7c = [("sa", 0)]

        bis_cnt_sv = [0]

        def emit_bisect_counts(first):
            if not first:
                wt("vector", "sa", sa_bis_box[0])
            if first:
                wt("vector", "sg", sg_setup)
            run("vector", "sv", lambda v: v.tensor_scalar(
                out=bis[:, 8:23], in0=ramp16[:, 0:15], scalar1=bis[:, 2:3],
                scalar2=bis[:, 0:1], op0=ALU.mult, op1=ALU.add))
            for j in range(15):
                run("vector", "sv", lambda v, j=j: v.tensor_scalar(
                    out=geD, in0=score_cols, scalar1=bis[:, 8 + j:9 + j],
                    scalar2=0.0, op0=ALU.is_ge, op1=ALU.add,
                    accum_out=csum16[:, j:j + 1]))
            bis_cnt_sv[0] = cnt["sv"]

        def emit_bisect_finish():
            wt("tensor", "sv", bis_cnt_sv[0])
            wt("tensor", pb7c[0][0], pb7c[0][1])
            run("tensor", "sp", lambda t: t.matmul(
                pb7[:, 0:15], ones128[:], csum16[:, 0:15], start=True, stop=True))
            sp_it = cnt["sp"]
            wt("scalar", "sp", sp_it)
            run("scalar", "sa", lambda s: s.activation(
                out=csum16[:, 0:15], in_=pb7[:, 0:15], func=AF.Identity))
            sa_bis_box[0] = cnt["sa"]
            pb7c[0] = ("sa", cnt["sa"])
            wt("vector", "sa", sa_bis_box[0])
            run("vector", "sv", lambda v: v.tensor_scalar(
                out=csum16[:, 0:15], in0=csum16[:, 0:15], scalar1=1024.0,
                scalar2=0.0, op0=ALU.is_ge, op1=ALU.add,
                accum_out=bis[:, 3:4]))
            run("vector", "sv", lambda v: v.scalar_tensor_tensor(
                out=bis[:, 0:1], in0=bis[:, 3:4], scalar=bis[:, 2:3],
                in1=bis[:, 0:1], op0=ALU.mult, op1=ALU.add))
            run("vector", "sv", lambda v: v.tensor_single_scalar(
                out=bis[:, 2:3], in_=bis[:, 2:3], scalar=1.0 / 16, op=ALU.mult))

        run("vector", "sv", lambda v: v.memset(lamt[:], 0.0))
        for ui in range(8):
            run("vector", "sv", lambda v, ui=ui: v.tensor_single_scalar(
                out=lamt[:, ui * TC:(ui + 1) * TC], in_=lamt[:, ui * TC:(ui + 1) * TC],
                scalar=dcols[:, 8 + ui:9 + ui], op=ALU.add))
        sv_lamt = cnt["sv"]

        rbank_free = [("sa", sa_boB)] * 3
        obank_free = [("sa", sa_boB)] * 4
        tr_free = {"kq": ("sv", sv_lamt), "vq": ("sv", sv_lamt),
                   "srq": ("sv", sv_lamt), "y1q": ("sv", sv_lamt),
                   "y2q": ("sv", sv_lamt),
                   "xatt0": ("sg", sg_setup), "xatt1": ("sg", sg_setup)}
        sv_xatt_half = [("sv", 0), ("sv", 0)]
        sp_rkv_end_v = [0]

        def emit_A(c, hh):
            u0 = hh * 4
            wtt("tensor", sw_att_h[hh])
            wt("tensor", "sa", sa_xnT_tile[2 * c + 1])
            groups = ([("k", 8 + u0 + j, j) for j in range(4)] +
                      [("v", 16 + u0 + j, j) for j in range(4)] +
                      [("r", 0 + u0 + j, j) for j in range(4)])
            for gidx, (kind, u, j) in enumerate(groups):
                bk = gidx % 3
                wt("tensor", rbank_free[bk][0], rbank_free[bk][1])
                for fi in range(8):
                    run("tensor", "sp", lambda t, u=u, fi=fi, c=c, bk=bk:
                        t.matmul(pb[bk][:, 0:TC],
                                 warena[:, AWRKV2 + fi * U3 + u * 128:
                                        AWRKV2 + fi * U3 + (u + 1) * 128],
                                 xnT[:, fi * L + c * TC:fi * L + (c + 1) * TC],
                                 start=(fi == 0), stop=(fi == 7)))
                sp_g = cnt["sp"]
                ucol = u0 + j
                wt("scalar", "sp", sp_g)
                if kind == "k":
                    wt("scalar", tr_free["kq"][0], tr_free["kq"][1])
                    run("scalar", "sa", lambda s, j=j, ucol=ucol, bk=bk:
                        s.activation(out=kq[:, j * TC:(j + 1) * TC],
                                     in_=pb[bk][:, 0:TC], func=AF.Exp,
                                     bias=pcols[:, 8 + ucol:9 + ucol]))
                elif kind == "v":
                    wt("scalar", tr_free["vq"][0], tr_free["vq"][1])
                    run("scalar", "sa", lambda s, j=j, ucol=ucol, bk=bk:
                        s.activation(out=vq[:, j * TC:(j + 1) * TC],
                                     in_=pb[bk][:, 0:TC], func=AF.Identity,
                                     bias=pcols[:, 16 + ucol:17 + ucol]))
                else:
                    wt("scalar", tr_free["srq"][0], tr_free["srq"][1])
                    run("scalar", "sa", lambda s, j=j, ucol=ucol, bk=bk:
                        s.activation(out=srq[:, j * TC:(j + 1) * TC],
                                     in_=pb[bk][:, 0:TC], func=AF.Exp,
                                     scale=f32(-1.0),
                                     bias=dcols[:, 0 + ucol:1 + ucol]))
                rbank_free[bk] = ("sa", cnt["sa"])
            sa_rkv = cnt["sa"]
            wt("vector", "sa", sa_rkv)
            wt("vector", tr_free["y1q"][0], tr_free["y1q"][1])
            wt("vector", tr_free["y2q"][0], tr_free["y2q"][1])
            run("vector", "sv", lambda v: v.tensor_mul(
                out=vq[:], in0=kq[:], in1=vq[:]))
            for j in range(4):
                ucol = u0 + j
                run("vector", "sv", lambda v, j=j, ucol=ucol: v.tensor_tensor_scan(
                    out=y1q[:, j * TC:(j + 1) * TC],
                    data0=lamt[:, ucol * TC:(ucol + 1) * TC],
                    data1=vq[:, j * TC:(j + 1) * TC],
                    initial=carry[:, ucol:ucol + 1],
                    op0=ALU.mult, op1=ALU.add))
                run("vector", "sv", lambda v, j=j, ucol=ucol: v.tensor_copy(
                    out=carry[:, ucol:ucol + 1],
                    in_=y1q[:, j * TC + TC - 1:j * TC + TC]))
            for j in range(4):
                ucol = u0 + j
                run("vector", "sv", lambda v, j=j, ucol=ucol: v.tensor_tensor_scan(
                    out=y2q[:, j * TC:(j + 1) * TC],
                    data0=lamt[:, ucol * TC:(ucol + 1) * TC],
                    data1=kq[:, j * TC:(j + 1) * TC],
                    initial=carry[:, 8 + ucol:9 + ucol],
                    op0=ALU.mult, op1=ALU.add))
                run("vector", "sv", lambda v, j=j, ucol=ucol: v.tensor_copy(
                    out=carry[:, 8 + ucol:9 + ucol],
                    in_=y2q[:, j * TC + TC - 1:j * TC + TC]))
            sv_scan = cnt["sv"]
            wt("scalar", "sa", sa_rkv)
            run("scalar", "sa", lambda s: s.activation(
                out=srq[:], in_=srq[:], func=AF.Ln, bias=f32(1.0)))
            run("scalar", "sa", lambda s: s.activation(
                out=srq[:], in_=srq[:], func=AF.Exp, scale=f32(-1.0)))
            sa_sr = cnt["sa"]
            wt("vector", "sv", sv_scan)
            for j in range(4):
                ucol = u0 + j
                run("vector", "sv", lambda v, j=j, ucol=ucol: v.scalar_tensor_tensor(
                    out=y1q[:, j * TC:(j + 1) * TC],
                    in0=vq[:, j * TC:(j + 1) * TC],
                    scalar=dcols[:, 16 + ucol:17 + ucol],
                    in1=y1q[:, j * TC:(j + 1) * TC],
                    op0=ALU.mult, op1=ALU.add))
            for j in range(4):
                ucol = u0 + j
                run("vector", "sv", lambda v, j=j, ucol=ucol: v.scalar_tensor_tensor(
                    out=y2q[:, j * TC:(j + 1) * TC],
                    in0=kq[:, j * TC:(j + 1) * TC],
                    scalar=dcols[:, 16 + ucol:17 + ucol],
                    in1=y2q[:, j * TC:(j + 1) * TC],
                    op0=ALU.mult, op1=ALU.add))
            sv_nd = cnt["sv"]
            tr_free["kq"] = ("sv", sv_nd)
            tr_free["vq"] = ("sv", sv_nd)
            wt("scalar", "sv", sv_nd)
            run("scalar", "sa", lambda s: s.activation(
                out=y2q[:], in_=y2q[:], func=AF.Ln, bias=EPS_DEN))
            run("scalar", "sa", lambda s: s.activation(
                out=y2q[:], in_=y2q[:], func=AF.Exp, scale=f32(-1.0)))
            sa_rec = cnt["sa"]
            wt("vector", "sa", sa_rec)
            run("vector", "sv", lambda v: v.tensor_mul(
                out=y1q[:], in0=y1q[:], in1=y2q[:]))
            key = f"xatt{hh}"
            wt("gpsimd", "sv", cnt["sv"])
            wt("gpsimd", tr_free[key][0], tr_free[key][1])
            run("gpsimd", "sg", lambda g, hh=hh: g.tensor_mul(
                out=xatt[:, hh * 4 * TC:(hh + 1) * 4 * TC],
                in0=y1q[:], in1=srq[:]))
            sv_xatt_half[hh] = ("sg", cnt["sg"])
            tr_free["y1q"] = ("sg", cnt["sg"])
            tr_free["y2q"] = ("sg", cnt["sg"])
            tr_free["srq"] = ("sg", cnt["sg"])

        def emit_W(c, hh):
            wtt("tensor", sw_att_wo)
            wt("tensor", sv_xatt_half[hh][0], sv_xatt_half[hh][1])
            for sub in range(2):
                for dh in range(2):
                    ob = sub * 2 + dh
                    if hh == 0:
                        wt("tensor", obank_free[ob][0], obank_free[ob][1])
                    for jj in range(4):
                        ui = hh * 4 + jj
                        run("tensor", "sp",
                            lambda t, ui=ui, sub=sub, dh=dh, ob=ob, hh=hh, jj=jj:
                            t.matmul(pb[3 + ob][:, 0:512],
                                     xatt[:, ui * TC + sub * 128:
                                          ui * TC + (sub + 1) * 128],
                                     warena[:, AWO2 + ui * D + dh * 512:
                                            AWO2 + ui * D + (dh + 1) * 512],
                                     start=(hh == 0 and jj == 0),
                                     stop=(hh == 1 and jj == 3)))
                    if hh == 1:
                        sp_o = cnt["sp"]
                        i_tile = c * 2 + sub
                        wt("vector", "sp", sp_o)
                        wt("vector", *sa_acc_tile[i_tile])
                        run("vector", "sv", lambda v, i=i_tile, dh=dh, ob=ob:
                            v.tensor_add(
                                out=acc[:, i * D + dh * 512:i * D + (dh + 1) * 512],
                                in0=acc[:, i * D + dh * 512:i * D + (dh + 1) * 512],
                                in1=pb[3 + ob][:, 0:512]))
                        obank_free[ob] = ("sv", cnt["sv"])
            tr_free[f"xatt{hh}"] = ("sp", cnt["sp"])


        bis_done = 0
        sa_acc_tile = {}
        sa_xnT_tile = {}
        sg_xn_t = {}
        xn_w_toks = []
        xnbf_tok = [None, None]
        xnbf_free = [("sp", 0), ("sp", 0)]

        def emit_tile_pool(i):
            accv = acc[:, i * D:(i + 1) * D]
            nb = i % 2
            wtt("gpsimd", [x_toks[i]])
            wt("gpsimd", xnbf_free[nb][0], xnbf_free[nb][1])
            if xnbf_tok[nb] is not None:
                wtt("gpsimd", [xnbf_tok[nb]])
            if i in (0, 8):
                wt("gpsimd", "sv", sv_colmath_h[i // 8])
            run("gpsimd", "sg", lambda g, accv=accv, i=i, nb=nb: g.tensor_scalar(
                out=xnbf[:, nb * D:(nb + 1) * D], in0=accv,
                scalar1=stat[:, S_RSTD + i:S_RSTD + i + 1],
                scalar2=stat[:, S_NMR + i:S_NMR + i + 1],
                op0=ALU.mult, op1=ALU.add))
            sg_xn_t[i] = cnt["sg"]
            wt("sync", "sg", sg_xn_t[i])
            xn_w_toks.append(dma("sync", "so", xn_dram[i * 128:(i + 1) * 128, 0:D],
                                 xnbf[:, nb * D:(nb + 1) * D]))
            run("gpsimd", "sg", lambda g, accv=accv, i=i: g.tensor_scalar(
                out=accv, in0=accv,
                scalar1=stat[:, S_R1P + i:S_R1P + i + 1],
                scalar2=stat[:, S_NMR + i:S_NMR + i + 1],
                op0=ALU.mult, op1=ALU.add))
            xnbf_tok[nb] = xn_w_toks[-1]

        def emit_tile_pool_add(i):
            if i == 0:
                wt("gpsimd", "sa", sa_boB)
            run("gpsimd", "sg", lambda g, i=i: g.tensor_add(
                out=acc[:, i * D:(i + 1) * D], in0=acc[:, i * D:(i + 1) * D],
                in1=bcB[:]))
            sa_acc_tile[i] = ("sg", cnt["sg"])

        emit_tile_pool(0)
        emit_tile_pool(1)
        for i in range(NT):
            nb = i % 2
            wt("tensor", "sg", sg_xn_t[i])
            wt("tensor", pb7c[0][0], pb7c[0][1])
            for fi in range(8):
                run("tensor", "sp", lambda t, fi=fi, nb=nb: t.matmul(
                    pb7_bf[:, fi * 128:(fi + 1) * 128],
                    xnbf[:, nb * D + fi * 128:nb * D + (fi + 1) * 128],
                    ident_b[:], start=(fi == 0), stop=(fi == 7),
                    is_transpose=True))
            sp_tr = cnt["sp"]
            xnbf_free[nb] = ("sp", sp_tr)
            wt("scalar", "sp", sp_tr)
            run("scalar", "sa", lambda s, i=i: s.activation(
                out=xnT.rearrange("p (fi t) -> p fi t", t=L)[
                    :, :, i * 128:(i + 1) * 128],
                in_=pb7_bf.rearrange("p (fi t) -> p fi t", t=128),
                func=AF.Identity))
            pb7c[0] = ("sa", cnt["sa"])
            sa_xnT_tile[i] = cnt["sa"]
            if i + 2 < NT:
                emit_tile_pool(i + 2)
            emit_tile_pool_add(i)
            target = (i + 1) * BISECT_ITERS // NT
            while bis_done < target:
                if bis_done > 0:
                    emit_bisect_finish()
                emit_bisect_counts(bis_done == 0)
                bis_done += 1
            if i % 2 == 1:
                c = (i - 1) // 2
                emit_A(c, 0)
                if c >= 1:
                    emit_W(c - 1, 1)
                emit_A(c, 1)
                if c == NCH - 1:
                    sp_rkv_end_v[0] = cnt["sp"]
                emit_W(c, 0)
        emit_W(NCH - 1, 1)
        emit_bisect_finish()
        sp_rkv_end = sp_rkv_end_v[0]
        sp_wo_end = cnt["sp"]
        sv_att_end = cnt["sv"]
        sg_att_end = cnt["sg"]
        sa_att_end = cnt["sa"]
        sa_sweep2 = cnt["sa"]
        sv_sweep2 = cnt["sv"]

        ckpt("ln")
        run("vector", "sv", lambda v: v.tensor_scalar(
            out=wsel, in0=score_cols, scalar1=bis[:, 0:1], scalar2=None,
            op0=ALU.is_ge))
        sv_mask = cnt["sv"]
        wt("scalar", "sv", sv_mask)
        run("scalar", "sa", lambda s: s.activation(
            out=sigT, in_=score_cols, func=AF.Exp, scale=f32(-1.0)))
        run("scalar", "sa", lambda s: s.activation(
            out=sigT, in_=sigT, func=AF.Ln, bias=f32(1.0)))
        run("scalar", "sa", lambda s: s.activation(
            out=sigT, in_=sigT, func=AF.Exp, scale=f32(-1.0)))
        sa_sig = cnt["sa"]
        wt("vector", "sa", sa_sig)
        run("vector", "sv", lambda v: v.tensor_mul(
            out=wsel, in0=wsel, in1=sigT))
        sv_wsel = cnt["sv"]

        run("vector", "sv", lambda v: v.tensor_scalar(
            out=geD, in0=score_cols, scalar1=bis[:, 0:1], scalar2=0.0,
            op0=ALU.is_ge, op1=ALU.add, accum_out=bis[:, 4:5]))
        sv_m = cnt["sv"]
        wt("tensor", "sv", sv_m)
        wt("tensor", pb7c[0][0], pb7c[0][1])
        run("tensor", "sp", lambda t: t.matmul(
            pb7[:, 0:1], tri_f[:], bis[:, 4:5], start=True, stop=True))
        wt("scalar", "sp", cnt["sp"])
        run("scalar", "sa", lambda s: s.activation(
            out=bis[:, 5:6], in_=pb7[:, 0:1], func=AF.Identity))
        sa_rb = cnt["sa"]
        pb7c[0] = ("sa", cnt["sa"])
        run("vector", "sv", lambda v: v.memset(ramp16[:], 1.0))
        run("vector", "sv", lambda v: v.tensor_tensor_scan(
            out=csum16[:], data0=ramp16[:], data1=geD, initial=0.0,
            op0=ALU.mult, op1=ALU.add))
        run("vector", "sv", lambda v: v.tensor_sub(
            out=csum16[:], in0=csum16[:], in1=geD))
        wt("vector", "sa", sa_rb)
        run("vector", "sv", lambda v: v.tensor_single_scalar(
            out=csum16[:], in_=csum16[:], scalar=bis[:, 5:6], op=ALU.add))
        run("vector", "sv", lambda v: v.tensor_scalar(
            out=pairs_all[:, 0:16], in0=geD, scalar1=-4096.0, scalar2=4096.0,
            op0=ALU.mult, op1=ALU.add))
        run("vector", "sv", lambda v: v.tensor_mul(
            out=csum16[:], in0=csum16[:], in1=geD))
        run("vector", "sv", lambda v: v.tensor_add(
            out=csum16[:], in0=csum16[:], in1=pairs_all[:, 0:16]))
        run("vector", "sv", lambda v: v.tensor_copy(out=offsI[:], in_=csum16[:]))
        pairs_bf = pairs_all[:].bitcast(BF16)
        whi_b = csum16[:, 0:8].bitcast(BF16)
        wlo_b = csum16[:, 8:16].bitcast(BF16)
        run("vector", "sv", lambda v: v.tensor_single_scalar(
            out=whi_b, in_=wsel, scalar=0.5, op=ALU.mult))
        run("vector", "sv", lambda v: v.scalar_tensor_tensor(
            out=geD, in0=wsel, scalar=0.5, in1=whi_b,
            op0=ALU.mult, op1=ALU.subtract))
        run("vector", "sv", lambda v: v.tensor_copy(out=wlo_b, in_=geD))
        pv4 = pairs_bf.rearrange("p (i four) -> p i four", four=4)
        run("vector", "sv", lambda v: v.tensor_copy(
            out=pv4[:, :, 0:1], in_=whi_b.unsqueeze(2)))
        run("vector", "sv", lambda v: v.tensor_copy(
            out=pv4[:, :, 1:2], in_=wlo_b.unsqueeze(2)))
        run("vector", "sv", lambda v: v.tensor_copy(
            out=pv4[:, :, 2:3], in_=ibf[:].unsqueeze(2)))
        run("vector", "sv", lambda v: v.tensor_copy(
            out=pv4[:, :, 3:4], in_=pbf[:].unsqueeze(2)))
        sv_offs = cnt["sv"]
        sel_toks = []
        wt("gpsimd", "sv", sv_offs)
        pv_b = pairs_bf
        wt("sync", "sv", sv_offs)
        pair_w_toks = []
        for i in range(NT):
            pair_w_toks.append(dma(
                "sync", "so", xn_dram[i * 128:(i + 1) * 128, D:D + 4],
                pv_b[:, 4 * i:4 * i + 4]))
        tar_bf = tarena[:].bitcast(BF16)
        stg4 = [bcB[:].bitcast(BF16)[:, 0:1028], xnbf[:, 0:1028],
                tar_bf[:, 0:1028], tar_bf[:, 1028:2056]]
        sct_free = [None] * 4
        wt("sync", *sa_acc_tile[NT - 1])
        wt("sync", "sv", sv_att_end)
        sel_toks = []
        for i in range(NT):
            b = i % 4
            wtt("sync", [xn_w_toks[i], pair_w_toks[i]])
            if sct_free[b] is not None:
                wtt("sync", [sct_free[b]])
            tok_l = dma("sync", "sx", stg4[b], xn_dram[i * 128:(i + 1) * 128, :])
            wtt("gpsimd", [tok_l])
            tok_s = dma_indirect_out(
                "gpsimd", "sw", xsel_dram, offsI[:, i:i + 1], stg4[b], L // 2 - 1)
            sel_toks.append(tok_s)
            sct_free[b] = tok_s

        xsT_view = warena[:, 0:8192].rearrange("p (fi s) -> p fi s", s=1024)
        stage_free = [("sp", 0)] * 4
        sv_wtsl = 0
        wtt("sync", sel_toks)
        wt("tensor", "sv", sv_att_end)
        trc = {7: pb7c[0], 6: obank_free[3]}
        trbf = {7: pb7_bf, 6: pb6_bf}
        for st in range(8):
            b = st % 4
            stg = stg4[b]
            bk = 7 if st % 2 == 0 else 6
            wt("sync", stage_free[b][0], stage_free[b][1])
            if sv_wtsl:
                wt("sync", "sv", sv_wtsl)
            if st < 2:
                wt("sync", "sp", sp_rkv_end)
            tok_s = dma("sync", "sx", stg, xsel_dram[st * 128:(st + 1) * 128, :])
            wtt("tensor", [tok_s])
            wt("tensor", trc[bk][0], trc[bk][1])
            for fi in range(8):
                run("tensor", "sp", lambda t, fi=fi, stg=stg, bk=bk: t.matmul(
                    trbf[bk][:, fi * 128:(fi + 1) * 128],
                    stg[:, fi * 128:(fi + 1) * 128],
                    ident_b[:], start=(fi == 0), stop=(fi == 7),
                    is_transpose=True))
            sp_st = cnt["sp"]
            stage_free[b] = ("sp", sp_st)
            wt("scalar", "sp", sp_st)
            run("scalar", "sa", lambda s, st=st, bk=bk: s.activation(
                out=xsT_view[:, :, st * 128:(st + 1) * 128],
                in_=trbf[bk].rearrange("p (fi t) -> p fi t", t=128),
                func=AF.Identity))
            trc[bk] = ("sa", cnt["sa"])
            wtt("vector", [tok_s])
            run("vector", "sv", lambda v, st=st, stg=stg: v.tensor_add(
                out=wt_sl[:, 2 * st:2 * st + 1],
                in0=stg[:, 1024:1025], in1=stg[:, 1025:1026]))
            run("vector", "sv", lambda v, st=st, stg=stg: v.scalar_tensor_tensor(
                out=wt_sl[:, 2 * st + 1:2 * st + 2],
                in0=stg[:, 1026:1027], scalar=128.0,
                in1=stg[:, 1027:1028], op0=ALU.mult, op1=ALU.add))
            sv_wtsl = cnt["sv"]
        sa_xsel = cnt["sa"]
        sp_xsel = cnt["sp"]
        run("vector", "sv", lambda v: v.tensor_copy(
            out=toffI[:], in_=wt_sl[:].rearrange(
                "p (i two) -> p i two", two=2)[:, :, 1:2].squeeze(2)))
        sv_toff = cnt["sv"]

        wt("gpsimd", "sp", sp_xsel)
        sw_b2 = [dma("gpsimd", "sw", prow1[:, :], b2_hbm.unsqueeze(0))]
        wtt("tensor", sw_b2)
        wt("tensor", "sa", sa_att_end)
        run("tensor", "sp", lambda t: t.matmul(
            pb3[:, 0:512], ones_f[:, :], prow1[:, 0:512], start=True, stop=True))
        run("tensor", "sp", lambda t: t.matmul(
            pb4[:, 0:512], ones_f[:, :], prow1[:, 512:1024], start=True, stop=True))
        sp_b2B = cnt["sp"]
        wt("scalar", "sp", sp_b2B)
        wt("scalar", "sp", sp_xsel)
        run("scalar", "sa", lambda s: s.activation(
            out=bcB[:, 0:512], in_=pb3[:, 0:512], func=AF.Identity))
        run("scalar", "sa", lambda s: s.activation(
            out=bcB[:, 512:1024], in_=pb4[:, 0:512], func=AF.Identity))
        sa_b2B = cnt["sa"]

        wt("vector", "sa", sa_b2B)
        for i in range(NT):
            run("vector", "sv", lambda v, i=i: v.scalar_tensor_tensor(
                out=acc[:, i * D:(i + 1) * D], in0=bcB[:],
                scalar=wsel[:, i:i + 1], in1=acc[:, i * D:(i + 1) * D],
                op0=ALU.mult, op1=ALU.add))
        sv_b2 = cnt["sv"]

        wt("scalar", "sv", sv_b2)
        prefill_toks = []
        for i in range(NT):
            prefill_toks.append(dma(
                "scalar", "sz", upd_dram[i * 128:(i + 1) * 128, :],
                acc[:, i * D:(i + 1) * D]))

        TFS = 512
        NTFS = 2

        def load_w(q, which, gate_sp):
            if gate_sp:
                wt("sync", "sp", gate_sp)
            wtt("sync", cast_w12[0:1] if which == 1 else cast_w12[1:2])
            base = W1B[q] if which == 1 else W2B[q]
            srcf = w1q_srcb if which == 1 else w2q_srcb
            return dma("sync", "sx", warena[:, base:base + 8192].rearrange(
                "p (fi F) -> p fi F", F=1024), srcf(q))

        sw_q = {0: [load_w(0, 1, sp_rkv_end), load_w(0, 2, sp_rkv_end)]}
        sw_q[1] = [load_w(1, 1, sp_wo_end)]
        sp_qdone = {}
        sp_h_done = {}
        hbank_free = [("sv", 0), ("sv", 0),
                      ("sa", sa_b2B), ("sa", sa_b2B)]
        ubank_free = [("sa", sa_b2B)] * 4
        gslot_free = [("sv", 0), ("sv", 0)]
        hslice_free = [("sv", 0)] * 8
        hbf_gate = [("sv", 0), ("sv", 0)]
        updsb_free = [("sv", 0), ("sv", 0)]
        upd_toks = []
        upd_prev = {}

        hbufS = tarena[:, 0:4096]
        hbf16S = tarena[:, 5120:7168].bitcast(BF16)
        HB = [hbf16S, xnT[:, 0:4096]]
        upd_sb2 = [tarena[:, 7168:8192], xnbf[:].bitcast(F32)]

        def emit_h_gelu(q, tc):
            wtt("tensor", sw_q[q])
            if q == 0 and tc == 0:
                wt("tensor", "sa", sa_xsel)
            for ffi in range(8):
                bk = ffi % 4
                wt("tensor", hbank_free[bk][0], hbank_free[bk][1])
                for fi in range(8):
                    run("tensor", "sp", lambda t, q=q, tc=tc, ffi=ffi, fi=fi, bk=bk:
                        t.matmul(pb[bk][:, 0:TFS],
                                 warena[:, W1B[q] + fi * 1024 + ffi * 128:
                                        W1B[q] + fi * 1024 + (ffi + 1) * 128],
                                 warena[:, fi * 1024 + tc * TFS:
                                        fi * 1024 + (tc + 1) * TFS],
                                 start=(fi == 0), stop=(fi == 7)))
                sp_h = cnt["sp"]
                gs = ffi % 2
                gv = gbufh[:, gs * 512:(gs + 1) * 512]
                hv = hbufS[:, ffi * TFS:(ffi + 1) * TFS]
                wt("scalar", "sp", sp_h)
                wt("scalar", hslice_free[ffi][0], hslice_free[ffi][1])
                if q == 0 and tc == 0 and ffi == 0:
                    wt("scalar", "sv", sv_att_end)
                    wt("scalar", "sg", sg_att_end)
                    wt("scalar", "sp", sp_xsel)
                run("scalar", "sa", lambda s, ffi=ffi, q=q, bk=bk, hv=hv: s.activation(
                    out=hv, in_=pb[bk][:, 0:TFS], func=AF.Identity,
                    bias=pcols[:, 40 + q * 8 + ffi:41 + q * 8 + ffi]))
                hbank_free[bk] = ("sa", cnt["sa"])
                wt("scalar", gslot_free[gs][0], gslot_free[gs][1])
                run("scalar", "sa", lambda s, hv=hv, gv=gv: s.activation(
                    out=gv, in_=hv, func=AF.Square, scale=GEL_A))
                sa_sq = cnt["sa"]
                wt("vector", "sa", sa_sq)
                run("vector", "sv", lambda v, hv=hv, gv=gv: v.scalar_tensor_tensor(
                    out=gv, in0=gv, scalar=1.0, in1=hv,
                    op0=ALU.add, op1=ALU.mult))
                sv_z = cnt["sv"]
                wt("scalar", "sv", sv_z)
                run("scalar", "sa", lambda s, gv=gv: s.activation(
                    out=gv, in_=gv, func=AF.Tanh, scale=GEL_B))
                sa_t = cnt["sa"]
                wt("vector", "sa", sa_t)
                wt("vector", hbf_gate[tc][0], hbf_gate[tc][1])
                if q == 0 and tc == 1 and ffi == 0:
                    wt("vector", "sp", sp_rkv_end)
                run("vector", "sv",
                    lambda v, hv=hv, gv=gv, ffi=ffi, tc=tc: v.scalar_tensor_tensor(
                        out=HB[tc][:, ffi * TFS:(ffi + 1) * TFS], in0=gv,
                        scalar=1.0, in1=hv, op0=ALU.add, op1=ALU.mult))
                sv_tt = cnt["sv"]
                gslot_free[gs] = ("sv", sv_tt)
                hslice_free[ffi] = ("sv", sv_tt)
            return cnt["sv"]

        def emit_upd(q, tc, sv_hbf):
            wt("tensor", "sv", sv_hbf)
            for sub in range(4):
                st = tc * 4 + sub
                for dh in range(2):
                    bank = 4 + (sub % 2) * 2 + dh
                    wt("tensor", ubank_free[bank - 4][0], ubank_free[bank - 4][1])
                    for ffi in range(8):
                        run("tensor", "sp",
                            lambda t, q=q, tc=tc, dh=dh, sub=sub, ffi=ffi,
                            bank=bank:
                            t.matmul(pb[bank][:, 0:512],
                                     HB[tc][:, ffi * TFS + sub * 128:
                                            ffi * TFS + (sub + 1) * 128],
                                     warena[:, W2B[q] + ffi * 1024 + dh * 512:
                                            W2B[q] + ffi * 1024 + (dh + 1) * 512],
                                     start=(ffi == 0), stop=(ffi == 7)))
                    sp_u = cnt["sp"]
                    wt("vector", "sp", sp_u)
                    if q == 0 and tc == 0 and sub == 0 and dh == 0:
                        wt("vector", "sv", sv_toff)
                        wt("vector", "sp", sp_wo_end)
                    if q == 0 and tc == 0 and sub == 1 and dh == 0:
                        wt("vector", "sp", sp_xsel)
                    ub2 = st % 2
                    wt("vector", updsb_free[ub2][0], updsb_free[ub2][1])
                    run("vector", "sv",
                        lambda v, st=st, dh=dh, bank=bank, ub2=ub2:
                        v.tensor_single_scalar(
                            out=upd_sb2[ub2][:, dh * 512:(dh + 1) * 512],
                            in_=pb[bank][:, 0:512],
                            scalar=wt_sl[:, 2 * st:2 * st + 1], op=ALU.mult))
                    ubank_free[bank - 4] = ("sv", cnt["sv"])
                sv_st = cnt["sv"]
                wt("gpsimd", "sv", sv_st)
                if q == 0:
                    wtt("gpsimd", prefill_toks)
                if st in upd_prev:
                    wtt("gpsimd", [upd_prev[st]])
                tok_u = dma_indirect_out(
                    "gpsimd", "sw", upd_dram, toffI[:, st:st + 1],
                    upd_sb2[st % 2], L - 1, cce=ALU.add)
                upd_toks.append(tok_u)
                upd_prev[st] = tok_u
                updsb_free[st % 2] = tok_u
            hbf_gate[tc] = ("sp", cnt["sp"])

        for q in range(NQ):
            sv_hbf = {}
            for tc in range(NTFS):
                sv_hbf[tc] = emit_h_gelu(q, tc)
            sp_h_done[q] = cnt["sp"]
            for tc in range(NTFS):
                emit_upd(q, tc, sv_hbf[tc])
            sp_qdone[q] = cnt["sp"]
            if q == 0:
                sw_q[1].append(load_w(1, 2, sp_h_done[0]))
            elif q == 1:
                sw_q[2] = [load_w(2, 1, sp_qdone[0]),
                           load_w(2, 2, sp_h_done[1])]
            elif q == 2:
                sw_q[3] = [load_w(3, 1, sp_qdone[1]),
                           load_w(3, 2, sp_h_done[2])]
        sv_ffn = cnt["sv"]
        sa_ffn = cnt["sa"]
        sp_ffn = cnt["sp"]

        so_toks = []
        wtt("sync", upd_toks)
        tins = [dma("sync", "sx", acc[:, i * D:(i + 1) * D],
                    upd_dram[i * 128:(i + 1) * 128, :]) for i in range(NT)]
        for i in range(NT):
            wtt("sync", [tins[i]])
            so_toks.append(dma("sync", "so", y_hbm[i * 128:(i + 1) * 128, :],
                               acc[:, i * D:(i + 1) * D]))
        ckpt("ffn")
        if debug:
            wt("sync", "sv", cnt["sv"])
            wt("sync", "sa", cnt["sa"])
            so_toks.append(dma("sync", "so", dbg["d_scores"], score_cols))
            so_toks.append(dma("sync", "so", dbg["d_thr"], bis[:, 0:4]))
            so_toks.append(dma("sync", "so", dbg["d_wsel"], wsel))
            so_toks.append(dma("sync", "so", dbg["d_acc"], acc[:]))
        wtt("sync", so_toks)

        @block.sync
        def _(e):
            for fn in plan["sync"]:
                fn(e)

        @block.gpsimd
        def _(e):
            for fn in plan["gpsimd"]:
                fn(e)

        @block.tensor
        def _(e):
            for fn in plan["tensor"]:
                fn(e)

        @block.scalar
        def _(e):
            for fn in plan["scalar"]:
                fn(e)

        @block.vector
        def _(e):
            for fn in plan["vector"]:
                fn(e)

    return nc


_NC_CACHE = {}


def kernel(**inputs):
    n = 8
    key = "prog"
    if key not in _NC_CACHE:
        _NC_CACHE[key] = build_program(debug=False)
    nc = _NC_CACHE[key]
    x = np.ascontiguousarray(inputs["L0_attn_inputs"], dtype=np.float32)
    common = {
        "W_rkv": np.ascontiguousarray(inputs["L0_attn_W_rkv"], np.float32),
        "b_rkv": np.ascontiguousarray(inputs["L0_attn_b_rkv"], np.float32),
        "W_o": np.ascontiguousarray(inputs["L0_attn_W_o"], np.float32),
        "b_o": np.ascontiguousarray(inputs["L0_attn_b_o"], np.float32),
        "nu_log": np.ascontiguousarray(inputs["L0_attn_nu_log"], np.float32),
        "gamma_log": np.ascontiguousarray(inputs["L0_attn_gamma_log"], np.float32),
        "w_router": np.ascontiguousarray(inputs["L0_ffn_w_router"], np.float32),
        "W1": np.ascontiguousarray(inputs["L0_ffn_W1"], np.float32),
        "b1": np.ascontiguousarray(inputs["L0_ffn_b1"], np.float32),
        "W2": np.ascontiguousarray(inputs["L0_ffn_W2"], np.float32),
        "b2": np.ascontiguousarray(inputs["L0_ffn_b2"], np.float32),
    }
    in_maps = [dict(common, x=np.ascontiguousarray(x[b])) for b in range(n)]
    res = run_bass_kernel_spmd(nc, in_maps, core_ids=list(range(n)))
    out = np.stack([r["y"] for r in res.results], axis=0)
    return out


# version 28
# speedup vs baseline: 1.0008x; 1.0008x over previous
import numpy as np
from contextlib import ExitStack
import concourse.bass as bass
import concourse.mybir as mybir
from concourse.bass import IndirectOffsetOnAxis
from concourse.bass_utils import run_bass_kernel_spmd

F32 = mybir.dt.float32
I32 = mybir.dt.int32
BF16 = mybir.dt.bfloat16
AF = mybir.ActivationFunctionType
ALU = mybir.AluOpType
AX = mybir.AxisListType
f32 = np.float32

L, D, U3, FFD = 2048, 1024, 3072, 4096
NT = L // 128
TC, NCH = 256, 8
TF, NTF = 512, 4
NQ = 4
GEL_A = f32(0.2114593)
GEL_B = f32(0.7978845608)
EPS_LN = f32(1e-5)
EPS_DEN = f32(1e-6)
BISECT_ITERS = 10
BIG = 1.0e5


def build_program(debug=False, stop_after="full"):
    nc = bass.Bass("TRN2", target_bir_lowering=False, debug=False)

    x_hbm = nc.dram_tensor("x", [L, D], F32, kind="ExternalInput").ap()
    Wrkv_hbm = nc.dram_tensor("W_rkv", [D, U3], F32, kind="ExternalInput").ap()
    brkv_hbm = nc.dram_tensor("b_rkv", [U3], F32, kind="ExternalInput").ap()
    Wo_hbm = nc.dram_tensor("W_o", [D, D], F32, kind="ExternalInput").ap()
    bo_hbm = nc.dram_tensor("b_o", [D], F32, kind="ExternalInput").ap()
    nu_hbm = nc.dram_tensor("nu_log", [D], F32, kind="ExternalInput").ap()
    gl_hbm = nc.dram_tensor("gamma_log", [D], F32, kind="ExternalInput").ap()
    wr_hbm = nc.dram_tensor("w_router", [D], F32, kind="ExternalInput").ap()
    W1_hbm = nc.dram_tensor("W1", [D, FFD], F32, kind="ExternalInput").ap()
    b1_hbm = nc.dram_tensor("b1", [FFD], F32, kind="ExternalInput").ap()
    W2_hbm = nc.dram_tensor("W2", [FFD, D], F32, kind="ExternalInput").ap()
    b2_hbm = nc.dram_tensor("b2", [D], F32, kind="ExternalInput").ap()
    y_hbm = nc.dram_tensor("y", [L, D], F32, kind="ExternalOutput").ap()
    xn_dram = nc.dram_tensor("xn_dram", [L, D + 4], BF16, kind="Internal").ap()
    xsel_dram = nc.dram_tensor("xsel_dram", [L // 2, D + 4], BF16, kind="Internal").ap()
    pairs_dram = nc.dram_tensor("pairs_dram", [L // 2, 2], F32, kind="Internal").ap()
    upd_dram = nc.dram_tensor("upd_dram", [L, D], F32, kind="Internal").ap()
    wbf = nc.dram_tensor("wbf", [12 * 1024 * 1024], BF16, kind="Internal").ap()
    zdram = nc.dram_tensor("zdram", [128, D], F32, kind="Internal").ap()
    W1_bf = wbf[0:D * FFD].rearrange("(a b) -> a b", b=FFD)
    W2_bf = wbf[D * FFD:2 * D * FFD].rearrange("(a b) -> a b", b=D)
    Wrkv_bf = wbf[2 * D * FFD:2 * D * FFD + D * U3].rearrange("(a b) -> a b", b=U3)
    Wo_bf = wbf[2 * D * FFD + D * U3:2 * D * FFD + D * U3 + D * D].rearrange(
        "(a b) -> a b", b=D)
    dbg = {}
    if debug:
        for nm, shp in [("d_scores", [128, NT]), ("d_thr", [128, 4]),
                        ("d_wsel", [128, NT]), ("d_acc", [128, NT * D])]:
            dbg[nm] = nc.dram_tensor(nm, shp, F32, kind="ExternalOutput").ap()

    plan = {"sync": [], "gpsimd": [], "tensor": [], "scalar": [], "vector": []}
    cnt = {"sp": 0, "sa": 0, "sv": 0, "sg": 0}
    for _q in ("sx", "sw", "so", "sz"):
        for _i in range(8):
            cnt[f"{_q}{_i}"] = 0
    sems = {}

    dead = [False]

    def op(eng, fn):
        if not dead[0]:
            plan[eng].append(fn)

    def wt(eng, sem, val):
        assert 0 <= val <= cnt[sem], (sem, val, cnt[sem])
        if val > 0:
            op(eng, lambda e, s=sem, v=val: e.wait_ge(sems[s], v))

    def inc(sem, n=1):
        cnt[sem] += n
        return cnt[sem]

    _OWN = {"vector": "sv", "scalar": "sa", "gpsimd": "sg"}

    def run(eng, sem, fn):
        own = _OWN.get(eng)
        if own is not None and cnt[own] > 0:
            wt(eng, own, cnt[own])
        inc(sem)
        def wrapped(e, fn=fn, sem=sem):
            fn(e).then_inc(sems[sem], 1)
        op(eng, wrapped)

    RINGS = {"sx": 8, "sw": 8, "so": 4, "sz": 4}
    _ring_idx = {"sx": 0, "sw": 0, "so": 0, "sz": 0}

    def dma(eng, queue, dst, src):
        slot = _ring_idx[queue] % RINGS[queue]
        _ring_idx[queue] += 1
        sem = f"{queue}{slot}"
        if cnt[sem] > 0:
            wt(eng, sem, cnt[sem])
        inc(sem, 16)
        tok = (sem, cnt[sem])
        def wrapped(e, dst=dst, src=src, sem=sem):
            e.dma_start(out=dst, in_=src).then_inc(sems[sem], 16)
        op(eng, wrapped)
        return tok

    _breg = {}

    def dma_indirect_out(eng, queue, out_t, off_ap, in_ap, bound, cce=None):
        if bound not in _breg:
            _breg[bound] = None
            def init(e, bound=bound):
                r = e.alloc_register(f"bc{bound}")
                e.reg_mov(r, bound)
                _breg[bound] = r
            op(eng, init)
        slot = _ring_idx[queue] % RINGS[queue]
        _ring_idx[queue] += 1
        sem = f"{queue}{slot}"
        if cnt[sem] > 0:
            wt(eng, sem, cnt[sem])
        inc(sem, 16)
        tok = (sem, cnt[sem])
        def wrapped(e, out_t=out_t, off_ap=off_ap, in_ap=in_ap, sem=sem,
                    bound=bound, cce=cce):
            kw = {}
            if cce is not None:
                kw["compute_op"] = cce
            e.indirect_dma_start(
                out=out_t, out_offset=IndirectOffsetOnAxis(ap=off_ap, axis=0),
                in_=in_ap, in_offset=None, bounds_check=_breg[bound],
                oob_is_err=False, **kw).then_inc(sems[sem], 16)
        op(eng, wrapped)
        return tok

    def wtt(eng, toks):
        best = {}
        for s, v in toks:
            best[s] = max(best.get(s, 0), v)
        for s, v in sorted(best.items()):
            wt(eng, s, v)

    with ExitStack() as _ctx:
        def _e(x):
            return _ctx.enter_context(x)
        acc = _e(nc.sbuf_tensor("acc", [128, NT * D], F32))
        xnT = _e(nc.sbuf_tensor("xnT", [128, 8 * L], BF16))
        warena = _e(nc.sbuf_tensor("warena", [128, 32768], BF16))
        tarena = _e(nc.sbuf_tensor("tarena", [128, 8192], F32))
        xnbf = _e(nc.sbuf_tensor("xnbf", [128, 2 * D], BF16))
        bcB = _e(nc.sbuf_tensor("bcB", [128, D], F32))
        pcols = _e(nc.sbuf_tensor("pcols", [128, 96], F32))
        dcols = _e(nc.sbuf_tensor("dcols", [128, 48], F32))
        stat = _e(nc.sbuf_tensor("stat", [128, 8 * NT], F32))
        bis = _e(nc.sbuf_tensor("bis", [128, 64], F32))
        ramp16 = _e(nc.sbuf_tensor("ramp16", [128, 16], F32))
        csum16 = _e(nc.sbuf_tensor("csum16", [128, 16], F32))
        tcols = _e(nc.sbuf_tensor("tcols", [128, 16], F32))
        offsI = _e(nc.sbuf_tensor("offsI", [128, 16], I32))
        toffI = _e(nc.sbuf_tensor("toffI", [128, 8], I32))
        wt_sl = _e(nc.sbuf_tensor("wt_sl", [128, 16], F32))
        pairs_all = _e(nc.sbuf_tensor("pairs_all", [128, 32], F32))
        ibf = _e(nc.sbuf_tensor("ibf", [128, 16], BF16))
        pbf = _e(nc.sbuf_tensor("pbf", [128, 16], BF16))
        tri_f = _e(nc.sbuf_tensor("tri_f", [128, 128], F32))
        qiota = _e(nc.sbuf_tensor("qiota", [128, 128], F32))
        carry = _e(nc.sbuf_tensor("carry", [128, 16], F32))
        ident_b = _e(nc.sbuf_tensor("ident_b", [128, 128], BF16))
        ident_f = _e(nc.sbuf_tensor("ident_f", [128, 128], F32))
        ones_f = _e(nc.sbuf_tensor("ones_f", [1, 128], F32))
        ones128 = _e(nc.sbuf_tensor("ones128", [128, 128], F32))
        praw = _e(nc.sbuf_tensor("praw", [64, 128], F32))
        pb0 = _e(nc.psum_tensor("pb0", [128, 512], F32))
        pb1 = _e(nc.psum_tensor("pb1", [128, 512], F32))
        pb2 = _e(nc.psum_tensor("pb2", [128, 512], F32))
        pb3 = _e(nc.psum_tensor("pb3", [128, 512], F32))
        pb4 = _e(nc.psum_tensor("pb4", [128, 512], F32))
        pb5 = _e(nc.psum_tensor("pb5", [128, 512], F32))
        pb6 = _e(nc.psum_tensor("pb6", [128, 512], F32))
        pb7 = _e(nc.psum_tensor("pb7", [128, 512], F32))
        for _q in ("sx", "sw", "so", "sz"):
            for _i in range(8):
                _nm = f"{_q}{_i}"
                sems[_nm] = _e(nc.semaphore(_nm))
        for _nm in ("sp", "sa", "sv", "sg"):
            sems[_nm] = _e(nc.semaphore(_nm))
        block = _e(nc.Block())
        pb = [pb0, pb1, pb2, pb3, pb4, pb5, pb6, pb7]
        pb7_bf = pb7[:].bitcast(BF16)
        pb6_bf = pb6[:].bitcast(BF16)

        hbuf = tarena[:, 0:4096]
        gbufh = tarena[:, 4096:6144]
        hbf16 = tarena[:, 6144:8192].bitcast(BF16)
        kq = tarena[:, 0:1024]
        vq = tarena[:, 1024:2048]
        srq = tarena[:, 2048:3072]
        y1q = tarena[:, 3072:4096]
        y2q = tarena[:, 4096:5120]
        lamt = tarena[:, 5120:7168]
        xatt = tarena[:, 7168:8192].bitcast(BF16)
        dumpA = tarena[:, 0:1024]
        dumpB = tarena[:, 1024:2048]
        xb = tarena[:, 2048:4096]
        prow1 = tarena[0:1, 4096:5120]
        geD = bis[:, 32:48]
        sigT = bis[:, 48:64]

        S_SUM, S_SQ, S_MEAN, S_VAR, S_RSTD, S_NMR, S_R1P, S_XW = (
            0, 16, 32, 48, 64, 80, 96, 112)
        score_cols = stat[:, S_XW:S_XW + NT]
        wsel = bis[:, 16:32]

        AW1, AW2, AWRKV, AWO = 0, 8192, 0, 24576


        def ckpt(name):
            if stop_after == name:
                emit_final()
                dead[0] = True

        def emit_final():
            for s in ("sv", "sa", "sp", "sg"):
                wt("sync", s, cnt[s])
            toks = []
            for c in range(NCH):
                toks.append(dma(
                    "sync", "so", y_hbm[c * TC:(c + 1) * TC, :].rearrange(
                        "(i p) f -> p i f", p=128),
                    acc[:, c * 2 * D:(c + 1) * 2 * D]
                    .rearrange("p (i f) -> p i f", f=D)))
            wtt("sync", toks)
        sw_small1 = [
            dma("gpsimd", "sw", praw[0:24, :], brkv_hbm.rearrange("(a b) -> a b", b=128)),
            dma("gpsimd", "sw", praw[32:40, :], nu_hbm.rearrange("(a b) -> a b", b=128))]
        sw_wr = [dma("gpsimd", "sw", prow1[:, :], wr_hbm.unsqueeze(0))]
        run("gpsimd", "sg", lambda g: g.memset(ident_b[:], 1.0))
        run("gpsimd", "sg", lambda g: g.affine_select(
            ident_b[:], ident_b[:], [[-1, 128]], ALU.is_equal, 0.0,
            base=0, channel_multiplier=1))
        run("gpsimd", "sg", lambda g: g.memset(ident_f[:], 1.0))
        run("gpsimd", "sg", lambda g: g.affine_select(
            ident_f[:], ident_f[:], [[-1, 128]], ALU.is_equal, 0.0,
            base=0, channel_multiplier=1))
        run("gpsimd", "sg", lambda g: g.memset(ones_f[:], 1.0))
        run("gpsimd", "sg", lambda g: g.memset(ones128[:], 1.0))
        run("gpsimd", "sg", lambda g: g.memset(carry[:], 0.0))
        run("gpsimd", "sg", lambda g: g.iota(
            ramp16[:], [[1, 16]], base=1, channel_multiplier=0,
            allow_small_or_imprecise_dtypes=True))
        run("gpsimd", "sg", lambda g: g.iota(
            tcols[:], [[128, 16]], base=0, channel_multiplier=1,
            allow_small_or_imprecise_dtypes=True))
        run("gpsimd", "sg", lambda g: g.iota(
            ibf[:], [[1, 16]], base=0, channel_multiplier=0,
            allow_small_or_imprecise_dtypes=True))
        run("gpsimd", "sg", lambda g: g.iota(
            pbf[:], [[0, 16]], base=0, channel_multiplier=1,
            allow_small_or_imprecise_dtypes=True))
        run("gpsimd", "sg", lambda g: g.iota(
            tri_f[:], [[0, 128]], base=0, channel_multiplier=1,
            allow_small_or_imprecise_dtypes=True))
        run("gpsimd", "sg", lambda g: g.iota(
            qiota[:], [[1, 128]], base=0, channel_multiplier=0,
            allow_small_or_imprecise_dtypes=True))
        wt("vector", "sg", cnt["sg"])
        run("vector", "sv", lambda v: v.tensor_tensor(
            out=tri_f[:], in0=tri_f[:], in1=qiota[:], op=ALU.is_lt))
        run("gpsimd", "sg", lambda g: g.memset(tarena[:, 7168:8192], 0.0))
        sg_setup = cnt["sg"]
        wtt("tensor", sw_small1)
        wt("tensor", "sg", sg_setup)
        run("tensor", "sp", lambda t: t.matmul(
            pb5[:, 0:24], praw[0:24, :], ident_f[0:24, 0:24],
            start=True, stop=True, is_transpose=True))
        run("tensor", "sp", lambda t: t.matmul(
            pb6[:, 0:8], praw[32:40, :], ident_f[32:40, 32:40],
            start=True, stop=True, is_transpose=True))
        sp_tr1 = cnt["sp"]
        wt("scalar", "sp", sp_tr1)
        run("scalar", "sa", lambda s: s.activation(
            out=pcols[:, 0:24], in_=pb5[:, 0:24], func=AF.Identity))
        run("scalar", "sa", lambda s: s.activation(
            out=pcols[:, 24:32], in_=pb6[:, 0:8], func=AF.Identity))
        sa_tr1 = cnt["sa"]
        wt("gpsimd", "sp", sp_tr1)
        sw_small2 = [
            dma("gpsimd", "sw", praw[0:8, :], gl_hbm.rearrange("(a b) -> a b", b=128)),
            dma("gpsimd", "sw", praw[32:64, :],
                b1_hbm.rearrange("(a b) -> a b", b=128))]
        wtt("tensor", sw_small2)
        wt("tensor", "sa", sa_tr1)
        run("tensor", "sp", lambda t: t.matmul(
            pb5[:, 0:8], praw[0:8, :], ident_f[0:8, 0:8],
            start=True, stop=True, is_transpose=True))
        run("tensor", "sp", lambda t: t.matmul(
            pb6[:, 0:32], praw[32:64, :], ident_f[32:64, 32:64],
            start=True, stop=True, is_transpose=True))
        sp_tr2 = cnt["sp"]
        wt("scalar", "sp", sp_tr2)
        run("scalar", "sa", lambda s: s.activation(
            out=pcols[:, 32:40], in_=pb5[:, 0:8], func=AF.Identity))
        run("scalar", "sa", lambda s: s.activation(
            out=pcols[:, 40:72], in_=pb6[:, 0:32], func=AF.Identity))
        sa_tr2 = cnt["sa"]

        wtt("tensor", sw_wr)
        wt("tensor", "sg", sg_setup)
        run("tensor", "sp", lambda t: t.matmul(
            pb3[:, 0:512], ones_f[:, :], prow1[:, 0:512], start=True, stop=True))
        run("tensor", "sp", lambda t: t.matmul(
            pb4[:, 0:512], ones_f[:, :], prow1[:, 512:1024], start=True, stop=True))
        sp_wrB = cnt["sp"]
        wt("scalar", "sp", sp_wrB)
        run("scalar", "sa", lambda s: s.activation(
            out=bcB[:, 0:512], in_=pb3[:, 0:512], func=AF.Identity))
        run("scalar", "sa", lambda s: s.activation(
            out=bcB[:, 512:1024], in_=pb4[:, 0:512], func=AF.Identity))
        sa_wrB = cnt["sa"]

        wt("vector", "sa", sa_tr1)
        run("vector", "sv", lambda v: v.tensor_single_scalar(
            out=dcols[:, 0:8], in_=pcols[:, 0:8], scalar=-1.0, op=ALU.mult))
        wt("scalar", "sv", cnt["sv"])
        run("scalar", "sa", lambda s: s.activation(
            out=dcols[:, 25:33], in_=pcols[:, 24:32], func=AF.Exp))
        run("scalar", "sa", lambda s: s.activation(
            out=dcols[:, 8:16], in_=dcols[:, 25:33], func=AF.Exp,
            scale=f32(-1.0)))
        sa_lam = cnt["sa"]
        wt("vector", "sa", sa_lam)
        run("vector", "sv", lambda v: v.tensor_add(
            out=dcols[:, 25:33], in0=pcols[:, 24:32], in1=pcols[:, 32:40]))
        wt("scalar", "sv", cnt["sv"])
        run("scalar", "sa", lambda s: s.activation(
            out=dcols[:, 16:24], in_=dcols[:, 25:33], func=AF.Exp))
        wt("vector", "sa", cnt["sa"])
        run("vector", "sv", lambda v: v.tensor_single_scalar(
            out=dcols[:, 16:24], in_=dcols[:, 16:24], scalar=1.0,
            op=ALU.subtract))
        sv_gam = cnt["sv"]

        ckpt("setup")
        wt("gpsimd", "sp", sp_wrB)
        sw_bo = [dma("gpsimd", "sw", prow1[:, :], bo_hbm.unsqueeze(0))]
        W1B = {0: 8192, 1: 24576, 2: 16384, 3: 8192}
        W2B = {0: 16384, 1: 8192, 2: 24576, 3: 16384}
        cast_rkv = {0: [], 1: []}
        for hh in (0, 1):
            for s3 in range(3):
                c0 = s3 * 1024 + hh * 512
                cast_rkv[hh].append(dma(
                    "gpsimd", "sw", Wrkv_bf[:, c0:c0 + 512],
                    Wrkv_hbm[:, c0:c0 + 512]))
        cast_wo = [dma("gpsimd", "sw", Wo_bf, Wo_hbm)]
        cast_w12 = [dma("gpsimd", "sw", W1_bf, W1_hbm),
                    dma("gpsimd", "sw", W2_bf, W2_hbm)]

        def w1q_srcb(q):
            return W1_bf[:, q * 1024:(q + 1) * 1024].rearrange(
                "(fi p) F -> p fi F", p=128)

        def w2q_srcb(q):
            return W2_bf[q * 1024:(q + 1) * 1024, :].rearrange(
                "(fi p) F -> p fi F", p=128)

        AWO2, AWRKV2 = 24576, 0

        sa_sweep1_h = {}
        x_toks = []
        for i in range(NT):
            accv = acc[:, i * D:(i + 1) * D]
            tok_x = dma("sync", "sx", accv, x_hbm[i * 128:(i + 1) * 128, :])
            x_toks.append(tok_x)
            wtt("scalar", [tok_x])
            run("scalar", "sa", lambda s, accv=accv, i=i: s.activation(
                out=dumpA, in_=accv, func=AF.Identity,
                accum_out=stat[:, S_SUM + i:S_SUM + i + 1]))
            if i in (7, 15):
                sa_sweep1_h[i // 8] = cnt["sa"]
            wtt("vector", [tok_x])
            if i == 0:
                wt("vector", "sa", sa_wrB)
            run("vector", "sv", lambda v, accv=accv, i=i: v.scalar_tensor_tensor(
                out=dumpB, in0=accv, scalar=1.0,
                in1=accv, op0=ALU.mult, op1=ALU.mult,
                accum_out=stat[:, S_SQ + i:S_SQ + i + 1]))
            run("vector", "sv", lambda v, accv=accv, i=i: v.scalar_tensor_tensor(
                out=dumpB, in0=accv, scalar=1.0,
                in1=bcB[:], op0=ALU.mult, op1=ALU.mult,
                accum_out=stat[:, S_XW + i:S_XW + i + 1]))
        sa_sweep1, sv_sweep1 = cnt["sa"], cnt["sv"]
        sw_att_h = {}
        for hh in (0, 1):
            wtt("sync", cast_rkv[hh])
            sw_att_h[hh] = [dma(
                "sync", "sx",
                warena[:, AWRKV2:AWRKV2 + 8 * U3].rearrange(
                    "p (fi u) -> p fi u", u=U3)[
                    :, :, s3 * 1024 + hh * 512:s3 * 1024 + hh * 512 + 512],
                Wrkv_bf.rearrange("(fi p) u -> p fi u", p=128)[
                    :, :, s3 * 1024 + hh * 512:s3 * 1024 + hh * 512 + 512])
                for s3 in range(3)]
        wtt("sync", cast_wo)
        sw_att_wo = [dma(
            "sync", "sx",
            warena[:, AWO2:AWO2 + 8 * D].rearrange("p (ui d) -> p ui d", d=D),
            Wo_bf.rearrange("(ui p) d -> p ui d", p=128))]

        ckpt("sweep1")
        sv_colmath_h = {}
        for h in (0, 1):
            cl, ch_ = h * 8, h * 8 + 8
            wt("vector", "sa", sa_sweep1_h[h])
            run("vector", "sv", lambda v, cl=cl, ch_=ch_: v.tensor_single_scalar(
                out=stat[:, S_MEAN + cl:S_MEAN + ch_],
                in_=stat[:, S_SUM + cl:S_SUM + ch_],
                scalar=1.0 / D, op=ALU.mult))
            run("vector", "sv", lambda v, cl=cl, ch_=ch_: v.tensor_single_scalar(
                out=stat[:, S_VAR + cl:S_VAR + ch_],
                in_=stat[:, S_SQ + cl:S_SQ + ch_],
                scalar=1.0 / D, op=ALU.mult))
            run("vector", "sv", lambda v, cl=cl, ch_=ch_: v.tensor_mul(
                out=stat[:, S_SUM + cl:S_SUM + ch_],
                in0=stat[:, S_MEAN + cl:S_MEAN + ch_],
                in1=stat[:, S_MEAN + cl:S_MEAN + ch_]))
            run("vector", "sv", lambda v, cl=cl, ch_=ch_: v.tensor_sub(
                out=stat[:, S_VAR + cl:S_VAR + ch_],
                in0=stat[:, S_VAR + cl:S_VAR + ch_],
                in1=stat[:, S_SUM + cl:S_SUM + ch_]))
            sv_var = cnt["sv"]
            wt("scalar", "sv", sv_var)
            run("scalar", "sa", lambda s, cl=cl, ch_=ch_: s.activation(
                out=stat[:, S_RSTD + cl:S_RSTD + ch_],
                in_=stat[:, S_VAR + cl:S_VAR + ch_],
                func=AF.Ln, bias=EPS_LN))
            run("scalar", "sa", lambda s, cl=cl, ch_=ch_: s.activation(
                out=stat[:, S_RSTD + cl:S_RSTD + ch_],
                in_=stat[:, S_RSTD + cl:S_RSTD + ch_],
                func=AF.Exp, scale=f32(-0.5)))
            sa_rstd = cnt["sa"]
            wt("vector", "sa", sa_rstd)
            run("vector", "sv", lambda v, cl=cl, ch_=ch_: v.scalar_tensor_tensor(
                out=stat[:, S_NMR + cl:S_NMR + ch_],
                in0=stat[:, S_MEAN + cl:S_MEAN + ch_],
                scalar=-1.0, in1=stat[:, S_RSTD + cl:S_RSTD + ch_],
                op0=ALU.mult, op1=ALU.mult))
            run("vector", "sv", lambda v, cl=cl, ch_=ch_: v.tensor_single_scalar(
                out=stat[:, S_R1P + cl:S_R1P + ch_],
                in_=stat[:, S_RSTD + cl:S_RSTD + ch_],
                scalar=1.0, op=ALU.add))
            if h == 0:
                run("vector", "sv", lambda v: v.tensor_reduce(
                    out=dcols[:, 24:25], in_=bcB[:], axis=AX.X, op=ALU.add))
            run("vector", "sv", lambda v, cl=cl, ch_=ch_: v.tensor_mul(
                out=stat[:, S_XW + cl:S_XW + ch_],
                in0=stat[:, S_XW + cl:S_XW + ch_],
                in1=stat[:, S_RSTD + cl:S_RSTD + ch_]))
            run("vector", "sv", lambda v, cl=cl, ch_=ch_: v.scalar_tensor_tensor(
                out=score_cols[:, cl:ch_], in0=stat[:, S_NMR + cl:S_NMR + ch_],
                scalar=dcols[:, 24:25], in1=stat[:, S_XW + cl:S_XW + ch_],
                op0=ALU.mult, op1=ALU.add))
            sv_colmath_h[h] = cnt["sv"]
        sv_colmath = cnt["sv"]

        ckpt("colmath")
        wtt("tensor", sw_bo)
        wt("tensor", "sa", sa_wrB)
        run("tensor", "sp", lambda t: t.matmul(
            pb3[:, 0:512], ones_f[:, :], prow1[:, 0:512], start=True, stop=True))
        run("tensor", "sp", lambda t: t.matmul(
            pb4[:, 0:512], ones_f[:, :], prow1[:, 512:1024], start=True, stop=True))
        sp_boB = cnt["sp"]
        wt("scalar", "sp", sp_boB)
        wt("scalar", "sv", sv_colmath)
        run("scalar", "sa", lambda s: s.activation(
            out=bcB[:, 0:512], in_=pb3[:, 0:512], func=AF.Identity))
        run("scalar", "sa", lambda s: s.activation(
            out=bcB[:, 512:1024], in_=pb4[:, 0:512], func=AF.Identity))
        sa_boB = cnt["sa"]

        run("vector", "sv", lambda v: v.memset(bis[:, 0:8], 0.0))
        run("vector", "sv", lambda v: v.memset(bis[:, 0:1], -BIG))
        run("vector", "sv", lambda v: v.memset(bis[:, 2:3], 2.0 * BIG / 16))
        sa_bis_box = [cnt["sa"]]

        pb7c = [("sa", 0)]

        bis_cnt_sv = [0]

        def emit_bisect_counts(first):
            if not first:
                wt("vector", "sa", sa_bis_box[0])
            if first:
                wt("vector", "sg", sg_setup)
            run("vector", "sv", lambda v: v.tensor_scalar(
                out=bis[:, 8:23], in0=ramp16[:, 0:15], scalar1=bis[:, 2:3],
                scalar2=bis[:, 0:1], op0=ALU.mult, op1=ALU.add))
            for j in range(15):
                run("vector", "sv", lambda v, j=j: v.tensor_scalar(
                    out=geD, in0=score_cols, scalar1=bis[:, 8 + j:9 + j],
                    scalar2=0.0, op0=ALU.is_ge, op1=ALU.add,
                    accum_out=csum16[:, j:j + 1]))
            bis_cnt_sv[0] = cnt["sv"]

        def emit_bisect_finish():
            wt("tensor", "sv", bis_cnt_sv[0])
            wt("tensor", pb7c[0][0], pb7c[0][1])
            run("tensor", "sp", lambda t: t.matmul(
                pb7[:, 0:15], ones128[:], csum16[:, 0:15], start=True, stop=True))
            sp_it = cnt["sp"]
            wt("scalar", "sp", sp_it)
            run("scalar", "sa", lambda s: s.activation(
                out=csum16[:, 0:15], in_=pb7[:, 0:15], func=AF.Identity))
            sa_bis_box[0] = cnt["sa"]
            pb7c[0] = ("sa", cnt["sa"])
            wt("vector", "sa", sa_bis_box[0])
            run("vector", "sv", lambda v: v.tensor_scalar(
                out=csum16[:, 0:15], in0=csum16[:, 0:15], scalar1=1024.0,
                scalar2=0.0, op0=ALU.is_ge, op1=ALU.add,
                accum_out=bis[:, 3:4]))
            run("vector", "sv", lambda v: v.scalar_tensor_tensor(
                out=bis[:, 0:1], in0=bis[:, 3:4], scalar=bis[:, 2:3],
                in1=bis[:, 0:1], op0=ALU.mult, op1=ALU.add))
            run("vector", "sv", lambda v: v.tensor_single_scalar(
                out=bis[:, 2:3], in_=bis[:, 2:3], scalar=1.0 / 16, op=ALU.mult))

        run("vector", "sv", lambda v: v.memset(lamt[:], 0.0))
        for ui in range(8):
            run("vector", "sv", lambda v, ui=ui: v.tensor_single_scalar(
                out=lamt[:, ui * TC:(ui + 1) * TC], in_=lamt[:, ui * TC:(ui + 1) * TC],
                scalar=dcols[:, 8 + ui:9 + ui], op=ALU.add))
        sv_lamt = cnt["sv"]

        rbank_free = [("sa", sa_boB)] * 3
        obank_free = [("sa", sa_boB)] * 4
        tr_free = {"kq": ("sv", sv_lamt), "vq": ("sv", sv_lamt),
                   "srq": ("sv", sv_lamt), "y1q": ("sv", sv_lamt),
                   "y2q": ("sv", sv_lamt),
                   "xatt0": ("sg", sg_setup), "xatt1": ("sg", sg_setup)}
        sv_xatt_half = [("sv", 0), ("sv", 0)]
        sp_rkv_end_v = [0]

        def emit_A(c, hh):
            u0 = hh * 4
            wtt("tensor", sw_att_h[hh])
            wt("tensor", "sa", sa_xnT_tile[2 * c + 1])
            groups = ([("k", 8 + u0 + j, j) for j in range(4)] +
                      [("v", 16 + u0 + j, j) for j in range(4)] +
                      [("r", 0 + u0 + j, j) for j in range(4)])
            for gidx, (kind, u, j) in enumerate(groups):
                bk = gidx % 3
                wt("tensor", rbank_free[bk][0], rbank_free[bk][1])
                for fi in range(8):
                    run("tensor", "sp", lambda t, u=u, fi=fi, c=c, bk=bk:
                        t.matmul(pb[bk][:, 0:TC],
                                 warena[:, AWRKV2 + fi * U3 + u * 128:
                                        AWRKV2 + fi * U3 + (u + 1) * 128],
                                 xnT[:, fi * L + c * TC:fi * L + (c + 1) * TC],
                                 start=(fi == 0), stop=(fi == 7)))
                sp_g = cnt["sp"]
                ucol = u0 + j
                wt("scalar", "sp", sp_g)
                if kind == "k":
                    wt("scalar", tr_free["kq"][0], tr_free["kq"][1])
                    run("scalar", "sa", lambda s, j=j, ucol=ucol, bk=bk:
                        s.activation(out=kq[:, j * TC:(j + 1) * TC],
                                     in_=pb[bk][:, 0:TC], func=AF.Exp,
                                     bias=pcols[:, 8 + ucol:9 + ucol]))
                elif kind == "v":
                    wt("scalar", tr_free["vq"][0], tr_free["vq"][1])
                    run("scalar", "sa", lambda s, j=j, ucol=ucol, bk=bk:
                        s.activation(out=vq[:, j * TC:(j + 1) * TC],
                                     in_=pb[bk][:, 0:TC], func=AF.Identity,
                                     bias=pcols[:, 16 + ucol:17 + ucol]))
                else:
                    wt("scalar", tr_free["srq"][0], tr_free["srq"][1])
                    run("scalar", "sa", lambda s, j=j, ucol=ucol, bk=bk:
                        s.activation(out=srq[:, j * TC:(j + 1) * TC],
                                     in_=pb[bk][:, 0:TC], func=AF.Exp,
                                     scale=f32(-1.0),
                                     bias=dcols[:, 0 + ucol:1 + ucol]))
                rbank_free[bk] = ("sa", cnt["sa"])
            sa_rkv = cnt["sa"]
            wt("vector", "sa", sa_rkv)
            wt("vector", tr_free["y1q"][0], tr_free["y1q"][1])
            wt("vector", tr_free["y2q"][0], tr_free["y2q"][1])
            run("vector", "sv", lambda v: v.tensor_mul(
                out=vq[:], in0=kq[:], in1=vq[:]))
            for j in range(4):
                ucol = u0 + j
                run("vector", "sv", lambda v, j=j, ucol=ucol: v.tensor_tensor_scan(
                    out=y1q[:, j * TC:(j + 1) * TC],
                    data0=lamt[:, ucol * TC:(ucol + 1) * TC],
                    data1=vq[:, j * TC:(j + 1) * TC],
                    initial=carry[:, ucol:ucol + 1],
                    op0=ALU.mult, op1=ALU.add))
                run("vector", "sv", lambda v, j=j, ucol=ucol: v.tensor_copy(
                    out=carry[:, ucol:ucol + 1],
                    in_=y1q[:, j * TC + TC - 1:j * TC + TC]))
            for j in range(4):
                ucol = u0 + j
                run("vector", "sv", lambda v, j=j, ucol=ucol: v.tensor_tensor_scan(
                    out=y2q[:, j * TC:(j + 1) * TC],
                    data0=lamt[:, ucol * TC:(ucol + 1) * TC],
                    data1=kq[:, j * TC:(j + 1) * TC],
                    initial=carry[:, 8 + ucol:9 + ucol],
                    op0=ALU.mult, op1=ALU.add))
                run("vector", "sv", lambda v, j=j, ucol=ucol: v.tensor_copy(
                    out=carry[:, 8 + ucol:9 + ucol],
                    in_=y2q[:, j * TC + TC - 1:j * TC + TC]))
            sv_scan = cnt["sv"]
            wt("scalar", "sa", sa_rkv)
            run("scalar", "sa", lambda s: s.activation(
                out=srq[:], in_=srq[:], func=AF.Ln, bias=f32(1.0)))
            run("scalar", "sa", lambda s: s.activation(
                out=srq[:], in_=srq[:], func=AF.Exp, scale=f32(-1.0)))
            sa_sr = cnt["sa"]
            wt("vector", "sv", sv_scan)
            for j in range(4):
                ucol = u0 + j
                run("vector", "sv", lambda v, j=j, ucol=ucol: v.scalar_tensor_tensor(
                    out=y1q[:, j * TC:(j + 1) * TC],
                    in0=vq[:, j * TC:(j + 1) * TC],
                    scalar=dcols[:, 16 + ucol:17 + ucol],
                    in1=y1q[:, j * TC:(j + 1) * TC],
                    op0=ALU.mult, op1=ALU.add))
            for j in range(4):
                ucol = u0 + j
                run("vector", "sv", lambda v, j=j, ucol=ucol: v.scalar_tensor_tensor(
                    out=y2q[:, j * TC:(j + 1) * TC],
                    in0=kq[:, j * TC:(j + 1) * TC],
                    scalar=dcols[:, 16 + ucol:17 + ucol],
                    in1=y2q[:, j * TC:(j + 1) * TC],
                    op0=ALU.mult, op1=ALU.add))
            sv_nd = cnt["sv"]
            tr_free["kq"] = ("sv", sv_nd)
            tr_free["vq"] = ("sv", sv_nd)
            wt("scalar", "sv", sv_nd)
            run("scalar", "sa", lambda s: s.activation(
                out=y2q[:], in_=y2q[:], func=AF.Ln, bias=EPS_DEN))
            run("scalar", "sa", lambda s: s.activation(
                out=y2q[:], in_=y2q[:], func=AF.Exp, scale=f32(-1.0)))
            sa_rec = cnt["sa"]
            wt("vector", "sa", sa_rec)
            run("vector", "sv", lambda v: v.tensor_mul(
                out=y1q[:], in0=y1q[:], in1=y2q[:]))
            key = f"xatt{hh}"
            wt("gpsimd", "sv", cnt["sv"])
            wt("gpsimd", tr_free[key][0], tr_free[key][1])
            run("gpsimd", "sg", lambda g, hh=hh: g.tensor_mul(
                out=xatt[:, hh * 4 * TC:(hh + 1) * 4 * TC],
                in0=y1q[:], in1=srq[:]))
            sv_xatt_half[hh] = ("sg", cnt["sg"])
            tr_free["y1q"] = ("sg", cnt["sg"])
            tr_free["y2q"] = ("sg", cnt["sg"])
            tr_free["srq"] = ("sg", cnt["sg"])

        def emit_W(c, hh):
            wtt("tensor", sw_att_wo)
            wt("tensor", sv_xatt_half[hh][0], sv_xatt_half[hh][1])
            for sub in range(2):
                for dh in range(2):
                    ob = sub * 2 + dh
                    if hh == 0:
                        wt("tensor", obank_free[ob][0], obank_free[ob][1])
                    for jj in range(4):
                        ui = hh * 4 + jj
                        run("tensor", "sp",
                            lambda t, ui=ui, sub=sub, dh=dh, ob=ob, hh=hh, jj=jj:
                            t.matmul(pb[3 + ob][:, 0:512],
                                     xatt[:, ui * TC + sub * 128:
                                          ui * TC + (sub + 1) * 128],
                                     warena[:, AWO2 + ui * D + dh * 512:
                                            AWO2 + ui * D + (dh + 1) * 512],
                                     start=(hh == 0 and jj == 0),
                                     stop=(hh == 1 and jj == 3)))
                    if hh == 1:
                        sp_o = cnt["sp"]
                        i_tile = c * 2 + sub
                        wt("vector", "sp", sp_o)
                        wt("vector", *sa_acc_tile[i_tile])
                        run("vector", "sv", lambda v, i=i_tile, dh=dh, ob=ob:
                            v.tensor_add(
                                out=acc[:, i * D + dh * 512:i * D + (dh + 1) * 512],
                                in0=acc[:, i * D + dh * 512:i * D + (dh + 1) * 512],
                                in1=pb[3 + ob][:, 0:512]))
                        obank_free[ob] = ("sv", cnt["sv"])
            tr_free[f"xatt{hh}"] = ("sp", cnt["sp"])


        bis_done = 0
        sa_acc_tile = {}
        sa_xnT_tile = {}
        sg_xn_t = {}
        xn_w_toks = []
        xnbf_tok = [None, None]
        xnbf_free = [("sp", 0), ("sp", 0)]

        def emit_tile_pool(i):
            accv = acc[:, i * D:(i + 1) * D]
            nb = i % 2
            wtt("gpsimd", [x_toks[i]])
            wt("gpsimd", xnbf_free[nb][0], xnbf_free[nb][1])
            if xnbf_tok[nb] is not None:
                wtt("gpsimd", [xnbf_tok[nb]])
            if i in (0, 8):
                wt("gpsimd", "sv", sv_colmath_h[i // 8])
            run("gpsimd", "sg", lambda g, accv=accv, i=i, nb=nb: g.tensor_scalar(
                out=xnbf[:, nb * D:(nb + 1) * D], in0=accv,
                scalar1=stat[:, S_RSTD + i:S_RSTD + i + 1],
                scalar2=stat[:, S_NMR + i:S_NMR + i + 1],
                op0=ALU.mult, op1=ALU.add))
            sg_xn_t[i] = cnt["sg"]
            wt("sync", "sg", sg_xn_t[i])
            xn_w_toks.append(dma("sync", "so", xn_dram[i * 128:(i + 1) * 128, 0:D],
                                 xnbf[:, nb * D:(nb + 1) * D]))
            run("gpsimd", "sg", lambda g, accv=accv, i=i: g.tensor_scalar(
                out=accv, in0=accv,
                scalar1=stat[:, S_R1P + i:S_R1P + i + 1],
                scalar2=stat[:, S_NMR + i:S_NMR + i + 1],
                op0=ALU.mult, op1=ALU.add))
            xnbf_tok[nb] = xn_w_toks[-1]

        def emit_tile_pool_add(i):
            if i == 0:
                wt("gpsimd", "sa", sa_boB)
            run("gpsimd", "sg", lambda g, i=i: g.tensor_add(
                out=acc[:, i * D:(i + 1) * D], in0=acc[:, i * D:(i + 1) * D],
                in1=bcB[:]))
            sa_acc_tile[i] = ("sg", cnt["sg"])

        emit_tile_pool(0)
        emit_tile_pool(1)
        for i in range(NT):
            nb = i % 2
            wt("tensor", "sg", sg_xn_t[i])
            wt("tensor", pb7c[0][0], pb7c[0][1])
            for fi in range(8):
                run("tensor", "sp", lambda t, fi=fi, nb=nb: t.matmul(
                    pb7_bf[:, fi * 128:(fi + 1) * 128],
                    xnbf[:, nb * D + fi * 128:nb * D + (fi + 1) * 128],
                    ident_b[:], start=(fi == 0), stop=(fi == 7),
                    is_transpose=True))
            sp_tr = cnt["sp"]
            xnbf_free[nb] = ("sp", sp_tr)
            wt("scalar", "sp", sp_tr)
            run("scalar", "sa", lambda s, i=i: s.activation(
                out=xnT.rearrange("p (fi t) -> p fi t", t=L)[
                    :, :, i * 128:(i + 1) * 128],
                in_=pb7_bf.rearrange("p (fi t) -> p fi t", t=128),
                func=AF.Identity))
            pb7c[0] = ("sa", cnt["sa"])
            sa_xnT_tile[i] = cnt["sa"]
            if i + 2 < NT:
                emit_tile_pool(i + 2)
            emit_tile_pool_add(i)
            target = (i + 1) * BISECT_ITERS // NT
            while bis_done < target:
                if bis_done > 0:
                    emit_bisect_finish()
                emit_bisect_counts(bis_done == 0)
                bis_done += 1
            if i % 2 == 1:
                c = (i - 1) // 2
                emit_A(c, 0)
                if c >= 1:
                    emit_W(c - 1, 1)
                emit_A(c, 1)
                if c == NCH - 1:
                    sp_rkv_end_v[0] = cnt["sp"]
                emit_W(c, 0)
        emit_W(NCH - 1, 1)
        emit_bisect_finish()
        sp_rkv_end = sp_rkv_end_v[0]
        sp_wo_end = cnt["sp"]
        sv_att_end = cnt["sv"]
        sg_att_end = cnt["sg"]
        sa_att_end = cnt["sa"]
        sa_sweep2 = cnt["sa"]
        sv_sweep2 = cnt["sv"]

        ckpt("ln")
        run("vector", "sv", lambda v: v.tensor_scalar(
            out=wsel, in0=score_cols, scalar1=bis[:, 0:1], scalar2=None,
            op0=ALU.is_ge))
        sv_mask = cnt["sv"]
        wt("scalar", "sv", sv_mask)
        run("scalar", "sa", lambda s: s.activation(
            out=sigT, in_=score_cols, func=AF.Exp, scale=f32(-1.0)))
        run("scalar", "sa", lambda s: s.activation(
            out=sigT, in_=sigT, func=AF.Ln, bias=f32(1.0)))
        run("scalar", "sa", lambda s: s.activation(
            out=sigT, in_=sigT, func=AF.Exp, scale=f32(-1.0)))
        sa_sig = cnt["sa"]
        wt("vector", "sa", sa_sig)
        run("vector", "sv", lambda v: v.tensor_mul(
            out=wsel, in0=wsel, in1=sigT))
        sv_wsel = cnt["sv"]

        run("vector", "sv", lambda v: v.tensor_scalar(
            out=geD, in0=score_cols, scalar1=bis[:, 0:1], scalar2=0.0,
            op0=ALU.is_ge, op1=ALU.add, accum_out=bis[:, 4:5]))
        sv_m = cnt["sv"]
        wt("tensor", "sv", sv_m)
        wt("tensor", pb7c[0][0], pb7c[0][1])
        run("tensor", "sp", lambda t: t.matmul(
            pb7[:, 0:1], tri_f[:], bis[:, 4:5], start=True, stop=True))
        wt("scalar", "sp", cnt["sp"])
        run("scalar", "sa", lambda s: s.activation(
            out=bis[:, 5:6], in_=pb7[:, 0:1], func=AF.Identity))
        sa_rb = cnt["sa"]
        pb7c[0] = ("sa", cnt["sa"])
        run("vector", "sv", lambda v: v.memset(ramp16[:], 1.0))
        run("vector", "sv", lambda v: v.tensor_tensor_scan(
            out=csum16[:], data0=ramp16[:], data1=geD, initial=0.0,
            op0=ALU.mult, op1=ALU.add))
        run("vector", "sv", lambda v: v.tensor_sub(
            out=csum16[:], in0=csum16[:], in1=geD))
        wt("vector", "sa", sa_rb)
        run("vector", "sv", lambda v: v.tensor_single_scalar(
            out=csum16[:], in_=csum16[:], scalar=bis[:, 5:6], op=ALU.add))
        run("vector", "sv", lambda v: v.tensor_scalar(
            out=pairs_all[:, 0:16], in0=geD, scalar1=-4096.0, scalar2=4096.0,
            op0=ALU.mult, op1=ALU.add))
        run("vector", "sv", lambda v: v.tensor_mul(
            out=csum16[:], in0=csum16[:], in1=geD))
        run("vector", "sv", lambda v: v.tensor_add(
            out=csum16[:], in0=csum16[:], in1=pairs_all[:, 0:16]))
        run("vector", "sv", lambda v: v.tensor_copy(out=offsI[:], in_=csum16[:]))
        pairs_bf = pairs_all[:].bitcast(BF16)
        whi_b = csum16[:, 0:8].bitcast(BF16)
        wlo_b = csum16[:, 8:16].bitcast(BF16)
        run("vector", "sv", lambda v: v.tensor_single_scalar(
            out=whi_b, in_=wsel, scalar=0.5, op=ALU.mult))
        run("vector", "sv", lambda v: v.scalar_tensor_tensor(
            out=geD, in0=wsel, scalar=0.5, in1=whi_b,
            op0=ALU.mult, op1=ALU.subtract))
        run("vector", "sv", lambda v: v.tensor_copy(out=wlo_b, in_=geD))
        pv4 = pairs_bf.rearrange("p (i four) -> p i four", four=4)
        run("vector", "sv", lambda v: v.tensor_copy(
            out=pv4[:, :, 0:1], in_=whi_b.unsqueeze(2)))
        run("vector", "sv", lambda v: v.tensor_copy(
            out=pv4[:, :, 1:2], in_=wlo_b.unsqueeze(2)))
        run("vector", "sv", lambda v: v.tensor_copy(
            out=pv4[:, :, 2:3], in_=ibf[:].unsqueeze(2)))
        run("vector", "sv", lambda v: v.tensor_copy(
            out=pv4[:, :, 3:4], in_=pbf[:].unsqueeze(2)))
        sv_offs = cnt["sv"]
        sel_toks = []
        wt("gpsimd", "sv", sv_offs)
        pv_b = pairs_bf
        wt("sync", "sv", sv_offs)
        pair_w_toks = []
        for i in range(NT):
            pair_w_toks.append(dma(
                "sync", "so", xn_dram[i * 128:(i + 1) * 128, D:D + 4],
                pv_b[:, 4 * i:4 * i + 4]))
        tar_bf = tarena[:].bitcast(BF16)
        stg4 = [bcB[:].bitcast(BF16)[:, 0:1028], xnbf[:, 0:1028],
                tar_bf[:, 0:1028], tar_bf[:, 1028:2056]]
        sct_free = [None] * 4
        wt("sync", *sa_acc_tile[NT - 1])
        wt("sync", "sv", sv_att_end)
        sel_toks = []
        for i in range(NT):
            b = i % 4
            wtt("sync", [xn_w_toks[i], pair_w_toks[i]])
            if sct_free[b] is not None:
                wtt("sync", [sct_free[b]])
            tok_l = dma("sync", "sx", stg4[b], xn_dram[i * 128:(i + 1) * 128, :])
            wtt("gpsimd", [tok_l])
            tok_s = dma_indirect_out(
                "gpsimd", "sw", xsel_dram, offsI[:, i:i + 1], stg4[b], L // 2 - 1)
            sel_toks.append(tok_s)
            sct_free[b] = tok_s

        xsT_view = warena[:, 0:8192].rearrange("p (fi s) -> p fi s", s=1024)
        stage_free = [("sp", 0)] * 4
        sv_wtsl = 0
        wtt("sync", sel_toks)
        wt("tensor", "sv", sv_att_end)
        trc = {7: pb7c[0], 6: obank_free[3]}
        trbf = {7: pb7_bf, 6: pb6_bf}
        for st in range(8):
            b = st % 4
            stg = stg4[b]
            bk = 7 if st % 2 == 0 else 6
            wt("sync", stage_free[b][0], stage_free[b][1])
            if sv_wtsl:
                wt("sync", "sv", sv_wtsl)
            if st < 2:
                wt("sync", "sp", sp_rkv_end)
            tok_s = dma("sync", "sx", stg, xsel_dram[st * 128:(st + 1) * 128, :])
            wtt("tensor", [tok_s])
            wt("tensor", trc[bk][0], trc[bk][1])
            for fi in range(8):
                run("tensor", "sp", lambda t, fi=fi, stg=stg, bk=bk: t.matmul(
                    trbf[bk][:, fi * 128:(fi + 1) * 128],
                    stg[:, fi * 128:(fi + 1) * 128],
                    ident_b[:], start=(fi == 0), stop=(fi == 7),
                    is_transpose=True))
            sp_st = cnt["sp"]
            stage_free[b] = ("sp", sp_st)
            wt("scalar", "sp", sp_st)
            run("scalar", "sa", lambda s, st=st, bk=bk: s.activation(
                out=xsT_view[:, :, st * 128:(st + 1) * 128],
                in_=trbf[bk].rearrange("p (fi t) -> p fi t", t=128),
                func=AF.Identity))
            trc[bk] = ("sa", cnt["sa"])
            wtt("vector", [tok_s])
            run("vector", "sv", lambda v, st=st, stg=stg: v.tensor_add(
                out=wt_sl[:, 2 * st:2 * st + 1],
                in0=stg[:, 1024:1025], in1=stg[:, 1025:1026]))
            run("vector", "sv", lambda v, st=st, stg=stg: v.scalar_tensor_tensor(
                out=wt_sl[:, 2 * st + 1:2 * st + 2],
                in0=stg[:, 1026:1027], scalar=128.0,
                in1=stg[:, 1027:1028], op0=ALU.mult, op1=ALU.add))
            sv_wtsl = cnt["sv"]
        sa_xsel = cnt["sa"]
        sp_xsel = cnt["sp"]
        run("vector", "sv", lambda v: v.tensor_copy(
            out=toffI[:], in_=wt_sl[:].rearrange(
                "p (i two) -> p i two", two=2)[:, :, 1:2].squeeze(2)))
        sv_toff = cnt["sv"]

        wt("gpsimd", "sp", sp_xsel)
        sw_b2 = [dma("gpsimd", "sw", prow1[:, :], b2_hbm.unsqueeze(0))]
        wtt("tensor", sw_b2)
        wt("tensor", "sa", sa_att_end)
        run("tensor", "sp", lambda t: t.matmul(
            pb3[:, 0:512], ones_f[:, :], prow1[:, 0:512], start=True, stop=True))
        run("tensor", "sp", lambda t: t.matmul(
            pb4[:, 0:512], ones_f[:, :], prow1[:, 512:1024], start=True, stop=True))
        sp_b2B = cnt["sp"]
        wt("scalar", "sp", sp_b2B)
        wt("scalar", "sp", sp_xsel)
        run("scalar", "sa", lambda s: s.activation(
            out=bcB[:, 0:512], in_=pb3[:, 0:512], func=AF.Identity))
        run("scalar", "sa", lambda s: s.activation(
            out=bcB[:, 512:1024], in_=pb4[:, 0:512], func=AF.Identity))
        sa_b2B = cnt["sa"]

        wt("vector", "sa", sa_b2B)
        for i in range(NT):
            run("vector", "sv", lambda v, i=i: v.scalar_tensor_tensor(
                out=acc[:, i * D:(i + 1) * D], in0=bcB[:],
                scalar=wsel[:, i:i + 1], in1=acc[:, i * D:(i + 1) * D],
                op0=ALU.mult, op1=ALU.add))
        sv_b2 = cnt["sv"]

        wt("scalar", "sv", sv_b2)
        prefill_toks = []
        for i in range(NT):
            prefill_toks.append(dma(
                "scalar", "sz", upd_dram[i * 128:(i + 1) * 128, :],
                acc[:, i * D:(i + 1) * D]))

        TFS = 512
        NTFS = 2

        def load_w(q, which, gate_sp):
            if gate_sp:
                wt("sync", "sp", gate_sp)
            wtt("sync", cast_w12[0:1] if which == 1 else cast_w12[1:2])
            base = W1B[q] if which == 1 else W2B[q]
            srcf = w1q_srcb if which == 1 else w2q_srcb
            return dma("sync", "sx", warena[:, base:base + 8192].rearrange(
                "p (fi F) -> p fi F", F=1024), srcf(q))

        sw_q = {0: [load_w(0, 1, sp_rkv_end), load_w(0, 2, sp_rkv_end)]}
        sw_q[1] = [load_w(1, 1, sp_wo_end)]
        sp_qdone = {}
        sp_h_done = {}
        hbank_free = [("sv", 0), ("sv", 0),
                      ("sa", sa_b2B), ("sa", sa_b2B)]
        ubank_free = [("sa", sa_b2B)] * 4
        gslot_free = [("sv", 0), ("sv", 0)]
        hslice_free = [("sv", 0)] * 8
        hbf_gate = [("sv", 0), ("sv", 0)]
        updsb_free = [("sv", 0), ("sv", 0)]
        upd_toks = []
        upd_prev = {}

        hbufS = tarena[:, 0:4096]
        hbf16S = tarena[:, 5120:7168].bitcast(BF16)
        HB = [hbf16S, xnT[:, 0:4096]]
        upd_sb2 = [tarena[:, 7168:8192], xnbf[:].bitcast(F32)]

        def emit_h_gelu(q, tc):
            wtt("tensor", sw_q[q])
            if q == 0 and tc == 0:
                wt("tensor", "sa", sa_xsel)
            for ffi in range(8):
                bk = ffi % 4
                wt("tensor", hbank_free[bk][0], hbank_free[bk][1])
                for fi in range(8):
                    run("tensor", "sp", lambda t, q=q, tc=tc, ffi=ffi, fi=fi, bk=bk:
                        t.matmul(pb[bk][:, 0:TFS],
                                 warena[:, W1B[q] + fi * 1024 + ffi * 128:
                                        W1B[q] + fi * 1024 + (ffi + 1) * 128],
                                 warena[:, fi * 1024 + tc * TFS:
                                        fi * 1024 + (tc + 1) * TFS],
                                 start=(fi == 0), stop=(fi == 7)))
                sp_h = cnt["sp"]
                gs = ffi % 2
                gv = gbufh[:, gs * 512:(gs + 1) * 512]
                hv = hbufS[:, ffi * TFS:(ffi + 1) * TFS]
                wt("scalar", "sp", sp_h)
                wt("scalar", hslice_free[ffi][0], hslice_free[ffi][1])
                if q == 0 and tc == 0 and ffi == 0:
                    wt("scalar", "sv", sv_att_end)
                    wt("scalar", "sg", sg_att_end)
                    wt("scalar", "sp", sp_xsel)
                run("scalar", "sa", lambda s, ffi=ffi, q=q, bk=bk, hv=hv: s.activation(
                    out=hv, in_=pb[bk][:, 0:TFS], func=AF.Identity,
                    bias=pcols[:, 40 + q * 8 + ffi:41 + q * 8 + ffi]))
                hbank_free[bk] = ("sa", cnt["sa"])
                wt("scalar", gslot_free[gs][0], gslot_free[gs][1])
                run("scalar", "sa", lambda s, hv=hv, gv=gv: s.activation(
                    out=gv, in_=hv, func=AF.Square, scale=GEL_A))
                sa_sq = cnt["sa"]
                wt("vector", "sa", sa_sq)
                run("vector", "sv", lambda v, hv=hv, gv=gv: v.scalar_tensor_tensor(
                    out=gv, in0=gv, scalar=1.0, in1=hv,
                    op0=ALU.add, op1=ALU.mult))
                sv_z = cnt["sv"]
                wt("scalar", "sv", sv_z)
                run("scalar", "sa", lambda s, gv=gv: s.activation(
                    out=gv, in_=gv, func=AF.Tanh, scale=GEL_B))
                sa_t = cnt["sa"]
                wt("vector", "sa", sa_t)
                wt("vector", hbf_gate[tc][0], hbf_gate[tc][1])
                if q == 0 and tc == 1 and ffi == 0:
                    wt("vector", "sp", sp_rkv_end)
                run("vector", "sv",
                    lambda v, hv=hv, gv=gv, ffi=ffi, tc=tc: v.scalar_tensor_tensor(
                        out=HB[tc][:, ffi * TFS:(ffi + 1) * TFS], in0=gv,
                        scalar=1.0, in1=hv, op0=ALU.add, op1=ALU.mult))
                sv_tt = cnt["sv"]
                gslot_free[gs] = ("sv", sv_tt)
                hslice_free[ffi] = ("sv", sv_tt)
            return cnt["sv"]

        def emit_upd(q, tc, sv_hbf):
            wt("tensor", "sv", sv_hbf)
            for sub in range(4):
                st = tc * 4 + sub
                for dh in range(2):
                    bank = 4 + (sub % 2) * 2 + dh
                    wt("tensor", ubank_free[bank - 4][0], ubank_free[bank - 4][1])
                    for ffi in range(8):
                        run("tensor", "sp",
                            lambda t, q=q, tc=tc, dh=dh, sub=sub, ffi=ffi,
                            bank=bank:
                            t.matmul(pb[bank][:, 0:512],
                                     HB[tc][:, ffi * TFS + sub * 128:
                                            ffi * TFS + (sub + 1) * 128],
                                     warena[:, W2B[q] + ffi * 1024 + dh * 512:
                                            W2B[q] + ffi * 1024 + (dh + 1) * 512],
                                     start=(ffi == 0), stop=(ffi == 7)))
                    sp_u = cnt["sp"]
                    wt("vector", "sp", sp_u)
                    if q == 0 and tc == 0 and sub == 0 and dh == 0:
                        wt("vector", "sv", sv_toff)
                        wt("vector", "sp", sp_wo_end)
                    if q == 0 and tc == 0 and sub == 1 and dh == 0:
                        wt("vector", "sp", sp_xsel)
                    ub2 = st % 2
                    wt("vector", updsb_free[ub2][0], updsb_free[ub2][1])
                    run("vector", "sv",
                        lambda v, st=st, dh=dh, bank=bank, ub2=ub2:
                        v.tensor_single_scalar(
                            out=upd_sb2[ub2][:, dh * 512:(dh + 1) * 512],
                            in_=pb[bank][:, 0:512],
                            scalar=wt_sl[:, 2 * st:2 * st + 1], op=ALU.mult))
                    ubank_free[bank - 4] = ("sv", cnt["sv"])
                sv_st = cnt["sv"]
                wt("gpsimd", "sv", sv_st)
                if q == 0:
                    wtt("gpsimd", prefill_toks)
                if st in upd_prev:
                    wtt("gpsimd", [upd_prev[st]])
                tok_u = dma_indirect_out(
                    "gpsimd", "sw", upd_dram, toffI[:, st:st + 1],
                    upd_sb2[st % 2], L - 1, cce=ALU.add)
                upd_toks.append(tok_u)
                upd_prev[st] = tok_u
                updsb_free[st % 2] = tok_u
            hbf_gate[tc] = ("sp", cnt["sp"])

        for q in range(NQ):
            sv_hbf = {}
            for tc in range(NTFS):
                sv_hbf[tc] = emit_h_gelu(q, tc)
            sp_h_done[q] = cnt["sp"]
            for tc in range(NTFS):
                emit_upd(q, tc, sv_hbf[tc])
            sp_qdone[q] = cnt["sp"]
            if q == 0:
                sw_q[1].append(load_w(1, 2, sp_h_done[0]))
            elif q == 1:
                sw_q[2] = [load_w(2, 1, sp_qdone[0]),
                           load_w(2, 2, sp_h_done[1])]
            elif q == 2:
                sw_q[3] = [load_w(3, 1, sp_qdone[1]),
                           load_w(3, 2, sp_h_done[2])]
        sv_ffn = cnt["sv"]
        sa_ffn = cnt["sa"]
        sp_ffn = cnt["sp"]

        so_toks = []
        wtt("scalar", upd_toks)
        tins = [dma("scalar", "sz", acc[:, i * D:(i + 1) * D],
                    upd_dram[i * 128:(i + 1) * 128, :]) for i in range(NT)]
        for i in range(NT):
            wtt("sync", [tins[i]])
            so_toks.append(dma("sync", "so", y_hbm[i * 128:(i + 1) * 128, :],
                               acc[:, i * D:(i + 1) * D]))
        ckpt("ffn")
        if debug:
            wt("sync", "sv", cnt["sv"])
            wt("sync", "sa", cnt["sa"])
            so_toks.append(dma("sync", "so", dbg["d_scores"], score_cols))
            so_toks.append(dma("sync", "so", dbg["d_thr"], bis[:, 0:4]))
            so_toks.append(dma("sync", "so", dbg["d_wsel"], wsel))
            so_toks.append(dma("sync", "so", dbg["d_acc"], acc[:]))
        wtt("sync", so_toks)

        @block.sync
        def _(e):
            for fn in plan["sync"]:
                fn(e)

        @block.gpsimd
        def _(e):
            for fn in plan["gpsimd"]:
                fn(e)

        @block.tensor
        def _(e):
            for fn in plan["tensor"]:
                fn(e)

        @block.scalar
        def _(e):
            for fn in plan["scalar"]:
                fn(e)

        @block.vector
        def _(e):
            for fn in plan["vector"]:
                fn(e)

    return nc


_NC_CACHE = {}


def kernel(**inputs):
    n = 8
    key = "prog"
    if key not in _NC_CACHE:
        _NC_CACHE[key] = build_program(debug=False)
    nc = _NC_CACHE[key]
    x = np.ascontiguousarray(inputs["L0_attn_inputs"], dtype=np.float32)
    common = {
        "W_rkv": np.ascontiguousarray(inputs["L0_attn_W_rkv"], np.float32),
        "b_rkv": np.ascontiguousarray(inputs["L0_attn_b_rkv"], np.float32),
        "W_o": np.ascontiguousarray(inputs["L0_attn_W_o"], np.float32),
        "b_o": np.ascontiguousarray(inputs["L0_attn_b_o"], np.float32),
        "nu_log": np.ascontiguousarray(inputs["L0_attn_nu_log"], np.float32),
        "gamma_log": np.ascontiguousarray(inputs["L0_attn_gamma_log"], np.float32),
        "w_router": np.ascontiguousarray(inputs["L0_ffn_w_router"], np.float32),
        "W1": np.ascontiguousarray(inputs["L0_ffn_W1"], np.float32),
        "b1": np.ascontiguousarray(inputs["L0_ffn_b1"], np.float32),
        "W2": np.ascontiguousarray(inputs["L0_ffn_W2"], np.float32),
        "b2": np.ascontiguousarray(inputs["L0_ffn_b2"], np.float32),
    }
    in_maps = [dict(common, x=np.ascontiguousarray(x[b])) for b in range(n)]
    res = run_bass_kernel_spmd(nc, in_maps, core_ids=list(range(n)))
    out = np.stack([r["y"] for r in res.results], axis=0)
    return out


# version 29
# speedup vs baseline: 1.0548x; 1.0539x over previous
import numpy as np
from contextlib import ExitStack
import concourse.bass as bass
import concourse.mybir as mybir
from concourse.bass import IndirectOffsetOnAxis
from concourse.bass_utils import run_bass_kernel_spmd

F32 = mybir.dt.float32
I32 = mybir.dt.int32
BF16 = mybir.dt.bfloat16
AF = mybir.ActivationFunctionType
ALU = mybir.AluOpType
AX = mybir.AxisListType
f32 = np.float32

L, D, U3, FFD = 2048, 1024, 3072, 4096
NT = L // 128
TC, NCH = 256, 8
TF, NTF = 512, 4
NQ = 4
GEL_A = f32(0.2114593)
GEL_B = f32(0.7978845608)
EPS_LN = f32(1e-5)
EPS_DEN = f32(1e-6)
BISECT_ITERS = 10
BIG = 1.0e5


def build_program(debug=False, stop_after="full"):
    nc = bass.Bass("TRN2", target_bir_lowering=False, debug=False)

    x_hbm = nc.dram_tensor("x", [L, D], F32, kind="ExternalInput").ap()
    Wrkv_hbm = nc.dram_tensor("W_rkv", [D, U3], F32, kind="ExternalInput").ap()
    brkv_hbm = nc.dram_tensor("b_rkv", [U3], F32, kind="ExternalInput").ap()
    Wo_hbm = nc.dram_tensor("W_o", [D, D], F32, kind="ExternalInput").ap()
    bo_hbm = nc.dram_tensor("b_o", [D], F32, kind="ExternalInput").ap()
    nu_hbm = nc.dram_tensor("nu_log", [D], F32, kind="ExternalInput").ap()
    gl_hbm = nc.dram_tensor("gamma_log", [D], F32, kind="ExternalInput").ap()
    wr_hbm = nc.dram_tensor("w_router", [D], F32, kind="ExternalInput").ap()
    W1_hbm = nc.dram_tensor("W1", [D, FFD], F32, kind="ExternalInput").ap()
    b1_hbm = nc.dram_tensor("b1", [FFD], F32, kind="ExternalInput").ap()
    W2_hbm = nc.dram_tensor("W2", [FFD, D], F32, kind="ExternalInput").ap()
    b2_hbm = nc.dram_tensor("b2", [D], F32, kind="ExternalInput").ap()
    y_hbm = nc.dram_tensor("y", [L, D], F32, kind="ExternalOutput").ap()
    xn_dram = nc.dram_tensor("xn_dram", [L, D + 4], BF16, kind="Internal").ap()
    xsel_dram = nc.dram_tensor("xsel_dram", [L // 2, D + 4], BF16, kind="Internal").ap()
    pairs_dram = nc.dram_tensor("pairs_dram", [L // 2, 2], F32, kind="Internal").ap()
    upd_dram = nc.dram_tensor("upd_dram", [L, D], F32, kind="Internal").ap()
    wbf = nc.dram_tensor("wbf", [12 * 1024 * 1024], BF16, kind="Internal").ap()
    zdram = nc.dram_tensor("zdram", [128, D], F32, kind="Internal").ap()
    W1_bf = wbf[0:D * FFD].rearrange("(a b) -> a b", b=FFD)
    W2_bf = wbf[D * FFD:2 * D * FFD].rearrange("(a b) -> a b", b=D)
    Wrkv_bf = wbf[2 * D * FFD:2 * D * FFD + D * U3].rearrange("(a b) -> a b", b=U3)
    Wo_bf = wbf[2 * D * FFD + D * U3:2 * D * FFD + D * U3 + D * D].rearrange(
        "(a b) -> a b", b=D)
    dbg = {}
    if debug:
        for nm, shp in [("d_scores", [128, NT]), ("d_thr", [128, 4]),
                        ("d_wsel", [128, NT]), ("d_acc", [128, NT * D])]:
            dbg[nm] = nc.dram_tensor(nm, shp, F32, kind="ExternalOutput").ap()

    plan = {"sync": [], "gpsimd": [], "tensor": [], "scalar": [], "vector": []}
    cnt = {"sp": 0, "sa": 0, "sv": 0, "sg": 0}
    for _q in ("sx", "sw", "so", "sz"):
        for _i in range(8):
            cnt[f"{_q}{_i}"] = 0
    sems = {}

    dead = [False]

    def op(eng, fn):
        if not dead[0]:
            plan[eng].append(fn)

    def wt(eng, sem, val):
        assert 0 <= val <= cnt[sem], (sem, val, cnt[sem])
        if val > 0:
            op(eng, lambda e, s=sem, v=val: e.wait_ge(sems[s], v))

    def inc(sem, n=1):
        cnt[sem] += n
        return cnt[sem]

    _OWN = {"vector": "sv", "scalar": "sa", "gpsimd": "sg"}

    def run(eng, sem, fn):
        own = _OWN.get(eng)
        if own is not None and cnt[own] > 0:
            wt(eng, own, cnt[own])
        inc(sem)
        def wrapped(e, fn=fn, sem=sem):
            fn(e).then_inc(sems[sem], 1)
        op(eng, wrapped)

    RINGS = {"sx": 8, "sw": 8, "so": 4, "sz": 4}
    _ring_idx = {"sx": 0, "sw": 0, "so": 0, "sz": 0}

    def dma(eng, queue, dst, src):
        slot = _ring_idx[queue] % RINGS[queue]
        _ring_idx[queue] += 1
        sem = f"{queue}{slot}"
        if cnt[sem] > 0:
            wt(eng, sem, cnt[sem])
        inc(sem, 16)
        tok = (sem, cnt[sem])
        def wrapped(e, dst=dst, src=src, sem=sem):
            e.dma_start(out=dst, in_=src).then_inc(sems[sem], 16)
        op(eng, wrapped)
        return tok

    _breg = {}

    def dma_indirect_out(eng, queue, out_t, off_ap, in_ap, bound, cce=None):
        if bound not in _breg:
            _breg[bound] = None
            def init(e, bound=bound):
                r = e.alloc_register(f"bc{bound}")
                e.reg_mov(r, bound)
                _breg[bound] = r
            op(eng, init)
        slot = _ring_idx[queue] % RINGS[queue]
        _ring_idx[queue] += 1
        sem = f"{queue}{slot}"
        if cnt[sem] > 0:
            wt(eng, sem, cnt[sem])
        inc(sem, 16)
        tok = (sem, cnt[sem])
        def wrapped(e, out_t=out_t, off_ap=off_ap, in_ap=in_ap, sem=sem,
                    bound=bound, cce=cce):
            kw = {}
            if cce is not None:
                kw["compute_op"] = cce
            e.indirect_dma_start(
                out=out_t, out_offset=IndirectOffsetOnAxis(ap=off_ap, axis=0),
                in_=in_ap, in_offset=None, bounds_check=_breg[bound],
                oob_is_err=False, **kw).then_inc(sems[sem], 16)
        op(eng, wrapped)
        return tok

    def wtt(eng, toks):
        best = {}
        for s, v in toks:
            best[s] = max(best.get(s, 0), v)
        for s, v in sorted(best.items()):
            wt(eng, s, v)

    with ExitStack() as _ctx:
        def _e(x):
            return _ctx.enter_context(x)
        acc = _e(nc.sbuf_tensor("acc", [128, NT * D], F32))
        xnT = _e(nc.sbuf_tensor("xnT", [128, 8 * L], BF16))
        warena = _e(nc.sbuf_tensor("warena", [128, 32768], BF16))
        tarena = _e(nc.sbuf_tensor("tarena", [128, 8192], F32))
        xnbf = _e(nc.sbuf_tensor("xnbf", [128, 2 * D], BF16))
        bcB = _e(nc.sbuf_tensor("bcB", [128, D], F32))
        pcols = _e(nc.sbuf_tensor("pcols", [128, 96], F32))
        dcols = _e(nc.sbuf_tensor("dcols", [128, 48], F32))
        stat = _e(nc.sbuf_tensor("stat", [128, 8 * NT], F32))
        bis = _e(nc.sbuf_tensor("bis", [128, 64], F32))
        ramp16 = _e(nc.sbuf_tensor("ramp16", [128, 16], F32))
        csum16 = _e(nc.sbuf_tensor("csum16", [128, 16], F32))
        tcols = _e(nc.sbuf_tensor("tcols", [128, 16], F32))
        offsI = _e(nc.sbuf_tensor("offsI", [128, 16], I32))
        toffI = _e(nc.sbuf_tensor("toffI", [128, 8], I32))
        wt_sl = _e(nc.sbuf_tensor("wt_sl", [128, 16], F32))
        pairs_all = _e(nc.sbuf_tensor("pairs_all", [128, 32], F32))
        ibf = _e(nc.sbuf_tensor("ibf", [128, 16], BF16))
        pbf = _e(nc.sbuf_tensor("pbf", [128, 16], BF16))
        tri_f = _e(nc.sbuf_tensor("tri_f", [128, 128], F32))
        qiota = _e(nc.sbuf_tensor("qiota", [128, 128], F32))
        carry = _e(nc.sbuf_tensor("carry", [128, 16], F32))
        ident_b = _e(nc.sbuf_tensor("ident_b", [128, 128], BF16))
        ident_f = _e(nc.sbuf_tensor("ident_f", [128, 128], F32))
        ones_f = _e(nc.sbuf_tensor("ones_f", [1, 128], F32))
        ones128 = _e(nc.sbuf_tensor("ones128", [128, 128], F32))
        praw = _e(nc.sbuf_tensor("praw", [64, 128], F32))
        pb0 = _e(nc.psum_tensor("pb0", [128, 512], F32))
        pb1 = _e(nc.psum_tensor("pb1", [128, 512], F32))
        pb2 = _e(nc.psum_tensor("pb2", [128, 512], F32))
        pb3 = _e(nc.psum_tensor("pb3", [128, 512], F32))
        pb4 = _e(nc.psum_tensor("pb4", [128, 512], F32))
        pb5 = _e(nc.psum_tensor("pb5", [128, 512], F32))
        pb6 = _e(nc.psum_tensor("pb6", [128, 512], F32))
        pb7 = _e(nc.psum_tensor("pb7", [128, 512], F32))
        for _q in ("sx", "sw", "so", "sz"):
            for _i in range(8):
                _nm = f"{_q}{_i}"
                sems[_nm] = _e(nc.semaphore(_nm))
        for _nm in ("sp", "sa", "sv", "sg"):
            sems[_nm] = _e(nc.semaphore(_nm))
        block = _e(nc.Block())
        pb = [pb0, pb1, pb2, pb3, pb4, pb5, pb6, pb7]
        pb7_bf = pb7[:].bitcast(BF16)
        pb6_bf = pb6[:].bitcast(BF16)

        hbuf = tarena[:, 0:4096]
        gbufh = tarena[:, 4096:6144]
        hbf16 = tarena[:, 6144:8192].bitcast(BF16)
        kq = tarena[:, 0:1024]
        vq = tarena[:, 1024:2048]
        srq = tarena[:, 2048:3072]
        y1q = tarena[:, 3072:4096]
        y2q = tarena[:, 4096:5120]
        lamt = tarena[:, 5120:7168]
        xatt = tarena[:, 7168:8192].bitcast(BF16)
        dumpA = tarena[:, 0:1024]
        dumpB = tarena[:, 1024:2048]
        xb = tarena[:, 2048:4096]
        prow1 = tarena[0:1, 4096:5120]
        geD = bis[:, 32:48]
        sigT = bis[:, 48:64]

        S_SUM, S_SQ, S_MEAN, S_VAR, S_RSTD, S_NMR, S_R1P, S_XW = (
            0, 16, 32, 48, 64, 80, 96, 112)
        score_cols = stat[:, S_XW:S_XW + NT]
        wsel = bis[:, 16:32]

        AW1, AW2, AWRKV, AWO = 0, 8192, 0, 24576


        def ckpt(name):
            if stop_after == name:
                emit_final()
                dead[0] = True

        def emit_final():
            for s in ("sv", "sa", "sp", "sg"):
                wt("sync", s, cnt[s])
            toks = []
            for c in range(NCH):
                toks.append(dma(
                    "sync", "so", y_hbm[c * TC:(c + 1) * TC, :].rearrange(
                        "(i p) f -> p i f", p=128),
                    acc[:, c * 2 * D:(c + 1) * 2 * D]
                    .rearrange("p (i f) -> p i f", f=D)))
            wtt("sync", toks)
        sw_small1 = [
            dma("gpsimd", "sw", praw[0:24, :], brkv_hbm.rearrange("(a b) -> a b", b=128)),
            dma("gpsimd", "sw", praw[32:40, :], nu_hbm.rearrange("(a b) -> a b", b=128))]
        sw_wr = [dma("gpsimd", "sw", prow1[:, :], wr_hbm.unsqueeze(0))]
        run("gpsimd", "sg", lambda g: g.memset(ident_b[:], 1.0))
        run("gpsimd", "sg", lambda g: g.affine_select(
            ident_b[:], ident_b[:], [[-1, 128]], ALU.is_equal, 0.0,
            base=0, channel_multiplier=1))
        run("gpsimd", "sg", lambda g: g.memset(ident_f[:], 1.0))
        run("gpsimd", "sg", lambda g: g.affine_select(
            ident_f[:], ident_f[:], [[-1, 128]], ALU.is_equal, 0.0,
            base=0, channel_multiplier=1))
        run("gpsimd", "sg", lambda g: g.memset(ones_f[:], 1.0))
        run("gpsimd", "sg", lambda g: g.memset(ones128[:], 1.0))
        run("gpsimd", "sg", lambda g: g.memset(carry[:], 0.0))
        run("gpsimd", "sg", lambda g: g.iota(
            ramp16[:], [[1, 16]], base=1, channel_multiplier=0,
            allow_small_or_imprecise_dtypes=True))
        run("gpsimd", "sg", lambda g: g.iota(
            tcols[:], [[128, 16]], base=0, channel_multiplier=1,
            allow_small_or_imprecise_dtypes=True))
        run("gpsimd", "sg", lambda g: g.iota(
            ibf[:], [[1, 16]], base=0, channel_multiplier=0,
            allow_small_or_imprecise_dtypes=True))
        run("gpsimd", "sg", lambda g: g.iota(
            pbf[:], [[0, 16]], base=0, channel_multiplier=1,
            allow_small_or_imprecise_dtypes=True))
        run("gpsimd", "sg", lambda g: g.iota(
            tri_f[:], [[0, 128]], base=0, channel_multiplier=1,
            allow_small_or_imprecise_dtypes=True))
        run("gpsimd", "sg", lambda g: g.iota(
            qiota[:], [[1, 128]], base=0, channel_multiplier=0,
            allow_small_or_imprecise_dtypes=True))
        wt("vector", "sg", cnt["sg"])
        run("vector", "sv", lambda v: v.tensor_tensor(
            out=tri_f[:], in0=tri_f[:], in1=qiota[:], op=ALU.is_lt))
        run("gpsimd", "sg", lambda g: g.memset(tarena[:, 7168:8192], 0.0))
        sg_setup = cnt["sg"]
        wtt("tensor", sw_small1)
        wt("tensor", "sg", sg_setup)
        run("tensor", "sp", lambda t: t.matmul(
            pb5[:, 0:24], praw[0:24, :], ident_f[0:24, 0:24],
            start=True, stop=True, is_transpose=True))
        run("tensor", "sp", lambda t: t.matmul(
            pb6[:, 0:8], praw[32:40, :], ident_f[32:40, 32:40],
            start=True, stop=True, is_transpose=True))
        sp_tr1 = cnt["sp"]
        wt("scalar", "sp", sp_tr1)
        run("scalar", "sa", lambda s: s.activation(
            out=pcols[:, 0:24], in_=pb5[:, 0:24], func=AF.Identity))
        run("scalar", "sa", lambda s: s.activation(
            out=pcols[:, 24:32], in_=pb6[:, 0:8], func=AF.Identity))
        sa_tr1 = cnt["sa"]
        wt("gpsimd", "sp", sp_tr1)
        sw_small2 = [
            dma("gpsimd", "sw", praw[0:8, :], gl_hbm.rearrange("(a b) -> a b", b=128)),
            dma("gpsimd", "sw", praw[32:64, :],
                b1_hbm.rearrange("(a b) -> a b", b=128))]
        wtt("tensor", sw_small2)
        wt("tensor", "sa", sa_tr1)
        run("tensor", "sp", lambda t: t.matmul(
            pb5[:, 0:8], praw[0:8, :], ident_f[0:8, 0:8],
            start=True, stop=True, is_transpose=True))
        run("tensor", "sp", lambda t: t.matmul(
            pb6[:, 0:32], praw[32:64, :], ident_f[32:64, 32:64],
            start=True, stop=True, is_transpose=True))
        sp_tr2 = cnt["sp"]
        wt("scalar", "sp", sp_tr2)
        run("scalar", "sa", lambda s: s.activation(
            out=pcols[:, 32:40], in_=pb5[:, 0:8], func=AF.Identity))
        run("scalar", "sa", lambda s: s.activation(
            out=pcols[:, 40:72], in_=pb6[:, 0:32], func=AF.Identity))
        sa_tr2 = cnt["sa"]

        wtt("tensor", sw_wr)
        wt("tensor", "sg", sg_setup)
        run("tensor", "sp", lambda t: t.matmul(
            pb3[:, 0:512], ones_f[:, :], prow1[:, 0:512], start=True, stop=True))
        run("tensor", "sp", lambda t: t.matmul(
            pb4[:, 0:512], ones_f[:, :], prow1[:, 512:1024], start=True, stop=True))
        sp_wrB = cnt["sp"]
        wt("scalar", "sp", sp_wrB)
        run("scalar", "sa", lambda s: s.activation(
            out=bcB[:, 0:512], in_=pb3[:, 0:512], func=AF.Identity))
        run("scalar", "sa", lambda s: s.activation(
            out=bcB[:, 512:1024], in_=pb4[:, 0:512], func=AF.Identity))
        sa_wrB = cnt["sa"]

        wt("vector", "sa", sa_tr1)
        run("vector", "sv", lambda v: v.tensor_single_scalar(
            out=dcols[:, 0:8], in_=pcols[:, 0:8], scalar=-1.0, op=ALU.mult))
        wt("scalar", "sv", cnt["sv"])
        run("scalar", "sa", lambda s: s.activation(
            out=dcols[:, 25:33], in_=pcols[:, 24:32], func=AF.Exp))
        run("scalar", "sa", lambda s: s.activation(
            out=dcols[:, 8:16], in_=dcols[:, 25:33], func=AF.Exp,
            scale=f32(-1.0)))
        sa_lam = cnt["sa"]
        wt("vector", "sa", sa_lam)
        run("vector", "sv", lambda v: v.tensor_add(
            out=dcols[:, 25:33], in0=pcols[:, 24:32], in1=pcols[:, 32:40]))
        wt("scalar", "sv", cnt["sv"])
        run("scalar", "sa", lambda s: s.activation(
            out=dcols[:, 16:24], in_=dcols[:, 25:33], func=AF.Exp))
        wt("vector", "sa", cnt["sa"])
        run("vector", "sv", lambda v: v.tensor_single_scalar(
            out=dcols[:, 16:24], in_=dcols[:, 16:24], scalar=1.0,
            op=ALU.subtract))
        sv_gam = cnt["sv"]

        ckpt("setup")
        wt("gpsimd", "sp", sp_wrB)
        sw_bo = [dma("gpsimd", "sw", prow1[:, :], bo_hbm.unsqueeze(0))]
        W1B = {0: 8192, 1: 24576, 2: 16384, 3: 8192}
        W2B = {0: 16384, 1: 8192, 2: 24576, 3: 16384}
        rkv_src3 = Wrkv_hbm.rearrange("d (s3 hh c) -> d s3 hh c", hh=2, c=512)
        cast_rkv = {}
        for hh in (0, 1):
            cast_rkv[hh] = [dma(
                "gpsimd", "sw",
                Wrkv_bf[:, hh * 1536:(hh + 1) * 1536].rearrange(
                    "d (s3 c) -> d s3 c", c=512),
                rkv_src3[:, :, hh, :])]
        cast_wo = [dma("gpsimd", "sw", Wo_bf, Wo_hbm)]
        cast_w12 = [dma("gpsimd", "sw", W1_bf, W1_hbm),
                    dma("gpsimd", "sw", W2_bf, W2_hbm)]

        def w1q_srcb(q):
            return W1_bf[:, q * 1024:(q + 1) * 1024].rearrange(
                "(fi p) F -> p fi F", p=128)

        def w2q_srcb(q):
            return W2_bf[q * 1024:(q + 1) * 1024, :].rearrange(
                "(fi p) F -> p fi F", p=128)

        AWO2, AWRKV2 = 24576, 0

        sa_sweep1_h = {}
        x_toks = []
        for i in range(NT):
            accv = acc[:, i * D:(i + 1) * D]
            tok_x = dma("sync", "sx", accv, x_hbm[i * 128:(i + 1) * 128, :])
            x_toks.append(tok_x)
            wtt("scalar", [tok_x])
            run("scalar", "sa", lambda s, accv=accv, i=i: s.activation(
                out=dumpA, in_=accv, func=AF.Identity,
                accum_out=stat[:, S_SUM + i:S_SUM + i + 1]))
            if i in (7, 15):
                sa_sweep1_h[i // 8] = cnt["sa"]
            wtt("vector", [tok_x])
            if i == 0:
                wt("vector", "sa", sa_wrB)
            run("vector", "sv", lambda v, accv=accv, i=i: v.scalar_tensor_tensor(
                out=dumpB, in0=accv, scalar=1.0,
                in1=accv, op0=ALU.mult, op1=ALU.mult,
                accum_out=stat[:, S_SQ + i:S_SQ + i + 1]))
            run("vector", "sv", lambda v, accv=accv, i=i: v.scalar_tensor_tensor(
                out=dumpB, in0=accv, scalar=1.0,
                in1=bcB[:], op0=ALU.mult, op1=ALU.mult,
                accum_out=stat[:, S_XW + i:S_XW + i + 1]))
        sa_sweep1, sv_sweep1 = cnt["sa"], cnt["sv"]
        sw_att_h = {}
        for hh in (0, 1):
            wtt("sync", cast_rkv[hh])
            sw_att_h[hh] = [dma(
                "sync", "sx",
                warena[:, AWRKV2:AWRKV2 + 8 * U3].rearrange(
                    "p (fi u) -> p fi u", u=U3)[
                    :, :, hh * 1536:(hh + 1) * 1536],
                Wrkv_bf[:, hh * 1536:(hh + 1) * 1536].rearrange(
                    "(fi p) u -> p fi u", p=128))]
        wtt("sync", cast_wo)
        sw_att_wo = [dma(
            "sync", "sx",
            warena[:, AWO2:AWO2 + 8 * D].rearrange("p (ui d) -> p ui d", d=D),
            Wo_bf.rearrange("(ui p) d -> p ui d", p=128))]

        ckpt("sweep1")
        sv_colmath_h = {}
        for h in (0, 1):
            cl, ch_ = h * 8, h * 8 + 8
            wt("vector", "sa", sa_sweep1_h[h])
            run("vector", "sv", lambda v, cl=cl, ch_=ch_: v.tensor_single_scalar(
                out=stat[:, S_MEAN + cl:S_MEAN + ch_],
                in_=stat[:, S_SUM + cl:S_SUM + ch_],
                scalar=1.0 / D, op=ALU.mult))
            run("vector", "sv", lambda v, cl=cl, ch_=ch_: v.tensor_single_scalar(
                out=stat[:, S_VAR + cl:S_VAR + ch_],
                in_=stat[:, S_SQ + cl:S_SQ + ch_],
                scalar=1.0 / D, op=ALU.mult))
            run("vector", "sv", lambda v, cl=cl, ch_=ch_: v.tensor_mul(
                out=stat[:, S_SUM + cl:S_SUM + ch_],
                in0=stat[:, S_MEAN + cl:S_MEAN + ch_],
                in1=stat[:, S_MEAN + cl:S_MEAN + ch_]))
            run("vector", "sv", lambda v, cl=cl, ch_=ch_: v.tensor_sub(
                out=stat[:, S_VAR + cl:S_VAR + ch_],
                in0=stat[:, S_VAR + cl:S_VAR + ch_],
                in1=stat[:, S_SUM + cl:S_SUM + ch_]))
            sv_var = cnt["sv"]
            wt("scalar", "sv", sv_var)
            run("scalar", "sa", lambda s, cl=cl, ch_=ch_: s.activation(
                out=stat[:, S_RSTD + cl:S_RSTD + ch_],
                in_=stat[:, S_VAR + cl:S_VAR + ch_],
                func=AF.Ln, bias=EPS_LN))
            run("scalar", "sa", lambda s, cl=cl, ch_=ch_: s.activation(
                out=stat[:, S_RSTD + cl:S_RSTD + ch_],
                in_=stat[:, S_RSTD + cl:S_RSTD + ch_],
                func=AF.Exp, scale=f32(-0.5)))
            sa_rstd = cnt["sa"]
            wt("vector", "sa", sa_rstd)
            run("vector", "sv", lambda v, cl=cl, ch_=ch_: v.scalar_tensor_tensor(
                out=stat[:, S_NMR + cl:S_NMR + ch_],
                in0=stat[:, S_MEAN + cl:S_MEAN + ch_],
                scalar=-1.0, in1=stat[:, S_RSTD + cl:S_RSTD + ch_],
                op0=ALU.mult, op1=ALU.mult))
            run("vector", "sv", lambda v, cl=cl, ch_=ch_: v.tensor_single_scalar(
                out=stat[:, S_R1P + cl:S_R1P + ch_],
                in_=stat[:, S_RSTD + cl:S_RSTD + ch_],
                scalar=1.0, op=ALU.add))
            if h == 0:
                run("vector", "sv", lambda v: v.tensor_reduce(
                    out=dcols[:, 24:25], in_=bcB[:], axis=AX.X, op=ALU.add))
            run("vector", "sv", lambda v, cl=cl, ch_=ch_: v.tensor_mul(
                out=stat[:, S_XW + cl:S_XW + ch_],
                in0=stat[:, S_XW + cl:S_XW + ch_],
                in1=stat[:, S_RSTD + cl:S_RSTD + ch_]))
            run("vector", "sv", lambda v, cl=cl, ch_=ch_: v.scalar_tensor_tensor(
                out=score_cols[:, cl:ch_], in0=stat[:, S_NMR + cl:S_NMR + ch_],
                scalar=dcols[:, 24:25], in1=stat[:, S_XW + cl:S_XW + ch_],
                op0=ALU.mult, op1=ALU.add))
            sv_colmath_h[h] = cnt["sv"]
        sv_colmath = cnt["sv"]

        ckpt("colmath")
        wtt("tensor", sw_bo)
        wt("tensor", "sa", sa_wrB)
        run("tensor", "sp", lambda t: t.matmul(
            pb3[:, 0:512], ones_f[:, :], prow1[:, 0:512], start=True, stop=True))
        run("tensor", "sp", lambda t: t.matmul(
            pb4[:, 0:512], ones_f[:, :], prow1[:, 512:1024], start=True, stop=True))
        sp_boB = cnt["sp"]
        wt("scalar", "sp", sp_boB)
        wt("scalar", "sv", sv_colmath)
        run("scalar", "sa", lambda s: s.activation(
            out=bcB[:, 0:512], in_=pb3[:, 0:512], func=AF.Identity))
        run("scalar", "sa", lambda s: s.activation(
            out=bcB[:, 512:1024], in_=pb4[:, 0:512], func=AF.Identity))
        sa_boB = cnt["sa"]

        run("vector", "sv", lambda v: v.memset(bis[:, 0:8], 0.0))
        run("vector", "sv", lambda v: v.memset(bis[:, 0:1], -BIG))
        run("vector", "sv", lambda v: v.memset(bis[:, 2:3], 2.0 * BIG / 16))
        sa_bis_box = [cnt["sa"]]

        pb7c = [("sa", 0)]

        bis_cnt_sv = [0]

        def emit_bisect_counts(first):
            if not first:
                wt("vector", "sa", sa_bis_box[0])
            if first:
                wt("vector", "sg", sg_setup)
            run("vector", "sv", lambda v: v.tensor_scalar(
                out=bis[:, 8:23], in0=ramp16[:, 0:15], scalar1=bis[:, 2:3],
                scalar2=bis[:, 0:1], op0=ALU.mult, op1=ALU.add))
            for j in range(15):
                run("vector", "sv", lambda v, j=j: v.tensor_scalar(
                    out=geD, in0=score_cols, scalar1=bis[:, 8 + j:9 + j],
                    scalar2=0.0, op0=ALU.is_ge, op1=ALU.add,
                    accum_out=csum16[:, j:j + 1]))
            bis_cnt_sv[0] = cnt["sv"]

        def emit_bisect_finish():
            wt("tensor", "sv", bis_cnt_sv[0])
            wt("tensor", pb7c[0][0], pb7c[0][1])
            run("tensor", "sp", lambda t: t.matmul(
                pb7[:, 0:15], ones128[:], csum16[:, 0:15], start=True, stop=True))
            sp_it = cnt["sp"]
            wt("scalar", "sp", sp_it)
            run("scalar", "sa", lambda s: s.activation(
                out=csum16[:, 0:15], in_=pb7[:, 0:15], func=AF.Identity))
            sa_bis_box[0] = cnt["sa"]
            pb7c[0] = ("sa", cnt["sa"])
            wt("vector", "sa", sa_bis_box[0])
            run("vector", "sv", lambda v: v.tensor_scalar(
                out=csum16[:, 0:15], in0=csum16[:, 0:15], scalar1=1024.0,
                scalar2=0.0, op0=ALU.is_ge, op1=ALU.add,
                accum_out=bis[:, 3:4]))
            run("vector", "sv", lambda v: v.scalar_tensor_tensor(
                out=bis[:, 0:1], in0=bis[:, 3:4], scalar=bis[:, 2:3],
                in1=bis[:, 0:1], op0=ALU.mult, op1=ALU.add))
            run("vector", "sv", lambda v: v.tensor_single_scalar(
                out=bis[:, 2:3], in_=bis[:, 2:3], scalar=1.0 / 16, op=ALU.mult))

        run("vector", "sv", lambda v: v.memset(lamt[:], 0.0))
        for ui in range(8):
            run("vector", "sv", lambda v, ui=ui: v.tensor_single_scalar(
                out=lamt[:, ui * TC:(ui + 1) * TC], in_=lamt[:, ui * TC:(ui + 1) * TC],
                scalar=dcols[:, 8 + ui:9 + ui], op=ALU.add))
        sv_lamt = cnt["sv"]

        rbank_free = [("sa", sa_boB)] * 3
        obank_free = [("sa", sa_boB)] * 4
        tr_free = {"kq": ("sv", sv_lamt), "vq": ("sv", sv_lamt),
                   "srq": ("sv", sv_lamt), "y1q": ("sv", sv_lamt),
                   "y2q": ("sv", sv_lamt),
                   "xatt0": ("sg", sg_setup), "xatt1": ("sg", sg_setup)}
        sv_xatt_half = [("sv", 0), ("sv", 0)]
        sp_rkv_end_v = [0]

        def emit_A(c, hh):
            u0 = hh * 4
            wtt("tensor", sw_att_h[hh])
            wt("tensor", "sa", sa_xnT_tile[2 * c + 1])
            groups = ([("k", 8 + u0 + j, j) for j in range(4)] +
                      [("v", 16 + u0 + j, j) for j in range(4)] +
                      [("r", 0 + u0 + j, j) for j in range(4)])
            for gidx, (kind, u, j) in enumerate(groups):
                bk = gidx % 3
                wt("tensor", rbank_free[bk][0], rbank_free[bk][1])
                woff = hh * 1536 + (u // 8) * 512 + (u % 4) * 128
                for fi in range(8):
                    run("tensor", "sp", lambda t, woff=woff, fi=fi, c=c, bk=bk:
                        t.matmul(pb[bk][:, 0:TC],
                                 warena[:, AWRKV2 + fi * U3 + woff:
                                        AWRKV2 + fi * U3 + woff + 128],
                                 xnT[:, fi * L + c * TC:fi * L + (c + 1) * TC],
                                 start=(fi == 0), stop=(fi == 7)))
                sp_g = cnt["sp"]
                ucol = u0 + j
                wt("scalar", "sp", sp_g)
                if kind == "k":
                    wt("scalar", tr_free["kq"][0], tr_free["kq"][1])
                    run("scalar", "sa", lambda s, j=j, ucol=ucol, bk=bk:
                        s.activation(out=kq[:, j * TC:(j + 1) * TC],
                                     in_=pb[bk][:, 0:TC], func=AF.Exp,
                                     bias=pcols[:, 8 + ucol:9 + ucol]))
                elif kind == "v":
                    wt("scalar", tr_free["vq"][0], tr_free["vq"][1])
                    run("scalar", "sa", lambda s, j=j, ucol=ucol, bk=bk:
                        s.activation(out=vq[:, j * TC:(j + 1) * TC],
                                     in_=pb[bk][:, 0:TC], func=AF.Identity,
                                     bias=pcols[:, 16 + ucol:17 + ucol]))
                else:
                    wt("scalar", tr_free["srq"][0], tr_free["srq"][1])
                    run("scalar", "sa", lambda s, j=j, ucol=ucol, bk=bk:
                        s.activation(out=srq[:, j * TC:(j + 1) * TC],
                                     in_=pb[bk][:, 0:TC], func=AF.Exp,
                                     scale=f32(-1.0),
                                     bias=dcols[:, 0 + ucol:1 + ucol]))
                rbank_free[bk] = ("sa", cnt["sa"])
            sa_rkv = cnt["sa"]
            wt("vector", "sa", sa_rkv)
            wt("vector", tr_free["y1q"][0], tr_free["y1q"][1])
            wt("vector", tr_free["y2q"][0], tr_free["y2q"][1])
            run("vector", "sv", lambda v: v.tensor_mul(
                out=vq[:], in0=kq[:], in1=vq[:]))
            for j in range(4):
                ucol = u0 + j
                run("vector", "sv", lambda v, j=j, ucol=ucol: v.tensor_tensor_scan(
                    out=y1q[:, j * TC:(j + 1) * TC],
                    data0=lamt[:, ucol * TC:(ucol + 1) * TC],
                    data1=vq[:, j * TC:(j + 1) * TC],
                    initial=carry[:, ucol:ucol + 1],
                    op0=ALU.mult, op1=ALU.add))
                run("vector", "sv", lambda v, j=j, ucol=ucol: v.tensor_copy(
                    out=carry[:, ucol:ucol + 1],
                    in_=y1q[:, j * TC + TC - 1:j * TC + TC]))
            for j in range(4):
                ucol = u0 + j
                run("vector", "sv", lambda v, j=j, ucol=ucol: v.tensor_tensor_scan(
                    out=y2q[:, j * TC:(j + 1) * TC],
                    data0=lamt[:, ucol * TC:(ucol + 1) * TC],
                    data1=kq[:, j * TC:(j + 1) * TC],
                    initial=carry[:, 8 + ucol:9 + ucol],
                    op0=ALU.mult, op1=ALU.add))
                run("vector", "sv", lambda v, j=j, ucol=ucol: v.tensor_copy(
                    out=carry[:, 8 + ucol:9 + ucol],
                    in_=y2q[:, j * TC + TC - 1:j * TC + TC]))
            sv_scan = cnt["sv"]
            wt("scalar", "sa", sa_rkv)
            run("scalar", "sa", lambda s: s.activation(
                out=srq[:], in_=srq[:], func=AF.Ln, bias=f32(1.0)))
            run("scalar", "sa", lambda s: s.activation(
                out=srq[:], in_=srq[:], func=AF.Exp, scale=f32(-1.0)))
            sa_sr = cnt["sa"]
            wt("vector", "sv", sv_scan)
            for j in range(4):
                ucol = u0 + j
                run("vector", "sv", lambda v, j=j, ucol=ucol: v.scalar_tensor_tensor(
                    out=y1q[:, j * TC:(j + 1) * TC],
                    in0=vq[:, j * TC:(j + 1) * TC],
                    scalar=dcols[:, 16 + ucol:17 + ucol],
                    in1=y1q[:, j * TC:(j + 1) * TC],
                    op0=ALU.mult, op1=ALU.add))
            for j in range(4):
                ucol = u0 + j
                run("vector", "sv", lambda v, j=j, ucol=ucol: v.scalar_tensor_tensor(
                    out=y2q[:, j * TC:(j + 1) * TC],
                    in0=kq[:, j * TC:(j + 1) * TC],
                    scalar=dcols[:, 16 + ucol:17 + ucol],
                    in1=y2q[:, j * TC:(j + 1) * TC],
                    op0=ALU.mult, op1=ALU.add))
            sv_nd = cnt["sv"]
            tr_free["kq"] = ("sv", sv_nd)
            tr_free["vq"] = ("sv", sv_nd)
            wt("scalar", "sv", sv_nd)
            run("scalar", "sa", lambda s: s.activation(
                out=y2q[:], in_=y2q[:], func=AF.Ln, bias=EPS_DEN))
            run("scalar", "sa", lambda s: s.activation(
                out=y2q[:], in_=y2q[:], func=AF.Exp, scale=f32(-1.0)))
            sa_rec = cnt["sa"]
            wt("vector", "sa", sa_rec)
            run("vector", "sv", lambda v: v.tensor_mul(
                out=y1q[:], in0=y1q[:], in1=y2q[:]))
            key = f"xatt{hh}"
            wt("gpsimd", "sv", cnt["sv"])
            wt("gpsimd", tr_free[key][0], tr_free[key][1])
            run("gpsimd", "sg", lambda g, hh=hh: g.tensor_mul(
                out=xatt[:, hh * 4 * TC:(hh + 1) * 4 * TC],
                in0=y1q[:], in1=srq[:]))
            sv_xatt_half[hh] = ("sg", cnt["sg"])
            tr_free["y1q"] = ("sg", cnt["sg"])
            tr_free["y2q"] = ("sg", cnt["sg"])
            tr_free["srq"] = ("sg", cnt["sg"])

        def emit_W(c, hh):
            wtt("tensor", sw_att_wo)
            wt("tensor", sv_xatt_half[hh][0], sv_xatt_half[hh][1])
            for sub in range(2):
                for dh in range(2):
                    ob = sub * 2 + dh
                    if hh == 0:
                        wt("tensor", obank_free[ob][0], obank_free[ob][1])
                    for jj in range(4):
                        ui = hh * 4 + jj
                        run("tensor", "sp",
                            lambda t, ui=ui, sub=sub, dh=dh, ob=ob, hh=hh, jj=jj:
                            t.matmul(pb[3 + ob][:, 0:512],
                                     xatt[:, ui * TC + sub * 128:
                                          ui * TC + (sub + 1) * 128],
                                     warena[:, AWO2 + ui * D + dh * 512:
                                            AWO2 + ui * D + (dh + 1) * 512],
                                     start=(hh == 0 and jj == 0),
                                     stop=(hh == 1 and jj == 3)))
                    if hh == 1:
                        sp_o = cnt["sp"]
                        i_tile = c * 2 + sub
                        wt("vector", "sp", sp_o)
                        wt("vector", *sa_acc_tile[i_tile])
                        run("vector", "sv", lambda v, i=i_tile, dh=dh, ob=ob:
                            v.tensor_add(
                                out=acc[:, i * D + dh * 512:i * D + (dh + 1) * 512],
                                in0=acc[:, i * D + dh * 512:i * D + (dh + 1) * 512],
                                in1=pb[3 + ob][:, 0:512]))
                        obank_free[ob] = ("sv", cnt["sv"])
            tr_free[f"xatt{hh}"] = ("sp", cnt["sp"])


        bis_done = 0
        sa_acc_tile = {}
        sa_xnT_tile = {}
        sg_xn_t = {}
        xn_w_toks = []
        xnbf_tok = [None, None]
        xnbf_free = [("sp", 0), ("sp", 0)]

        def emit_tile_pool(i):
            accv = acc[:, i * D:(i + 1) * D]
            nb = i % 2
            wtt("gpsimd", [x_toks[i]])
            wt("gpsimd", xnbf_free[nb][0], xnbf_free[nb][1])
            if xnbf_tok[nb] is not None:
                wtt("gpsimd", [xnbf_tok[nb]])
            if i in (0, 8):
                wt("gpsimd", "sv", sv_colmath_h[i // 8])
            run("gpsimd", "sg", lambda g, accv=accv, i=i, nb=nb: g.tensor_scalar(
                out=xnbf[:, nb * D:(nb + 1) * D], in0=accv,
                scalar1=stat[:, S_RSTD + i:S_RSTD + i + 1],
                scalar2=stat[:, S_NMR + i:S_NMR + i + 1],
                op0=ALU.mult, op1=ALU.add))
            sg_xn_t[i] = cnt["sg"]
            wt("sync", "sg", sg_xn_t[i])
            xn_w_toks.append(dma("sync", "so", xn_dram[i * 128:(i + 1) * 128, 0:D],
                                 xnbf[:, nb * D:(nb + 1) * D]))
            run("gpsimd", "sg", lambda g, accv=accv, i=i: g.tensor_scalar(
                out=accv, in0=accv,
                scalar1=stat[:, S_R1P + i:S_R1P + i + 1],
                scalar2=stat[:, S_NMR + i:S_NMR + i + 1],
                op0=ALU.mult, op1=ALU.add))
            xnbf_tok[nb] = xn_w_toks[-1]

        def emit_tile_pool_add(i):
            if i == 0:
                wt("gpsimd", "sa", sa_boB)
            run("gpsimd", "sg", lambda g, i=i: g.tensor_add(
                out=acc[:, i * D:(i + 1) * D], in0=acc[:, i * D:(i + 1) * D],
                in1=bcB[:]))
            sa_acc_tile[i] = ("sg", cnt["sg"])

        emit_tile_pool(0)
        emit_tile_pool(1)
        for i in range(NT):
            nb = i % 2
            wt("tensor", "sg", sg_xn_t[i])
            wt("tensor", pb7c[0][0], pb7c[0][1])
            for fi in range(8):
                run("tensor", "sp", lambda t, fi=fi, nb=nb: t.matmul(
                    pb7_bf[:, fi * 128:(fi + 1) * 128],
                    xnbf[:, nb * D + fi * 128:nb * D + (fi + 1) * 128],
                    ident_b[:], start=(fi == 0), stop=(fi == 7),
                    is_transpose=True))
            sp_tr = cnt["sp"]
            xnbf_free[nb] = ("sp", sp_tr)
            wt("scalar", "sp", sp_tr)
            run("scalar", "sa", lambda s, i=i: s.activation(
                out=xnT.rearrange("p (fi t) -> p fi t", t=L)[
                    :, :, i * 128:(i + 1) * 128],
                in_=pb7_bf.rearrange("p (fi t) -> p fi t", t=128),
                func=AF.Identity))
            pb7c[0] = ("sa", cnt["sa"])
            sa_xnT_tile[i] = cnt["sa"]
            if i + 2 < NT:
                emit_tile_pool(i + 2)
            emit_tile_pool_add(i)
            target = (i + 1) * BISECT_ITERS // NT
            while bis_done < target:
                if bis_done > 0:
                    emit_bisect_finish()
                emit_bisect_counts(bis_done == 0)
                bis_done += 1
            if i % 2 == 1:
                c = (i - 1) // 2
                emit_A(c, 0)
                if c >= 1:
                    emit_W(c - 1, 1)
                emit_A(c, 1)
                if c == NCH - 1:
                    sp_rkv_end_v[0] = cnt["sp"]
                emit_W(c, 0)
        emit_W(NCH - 1, 1)
        emit_bisect_finish()
        sp_rkv_end = sp_rkv_end_v[0]
        sp_wo_end = cnt["sp"]
        sv_att_end = cnt["sv"]
        sg_att_end = cnt["sg"]
        sa_att_end = cnt["sa"]
        sa_sweep2 = cnt["sa"]
        sv_sweep2 = cnt["sv"]

        ckpt("ln")
        run("vector", "sv", lambda v: v.tensor_scalar(
            out=wsel, in0=score_cols, scalar1=bis[:, 0:1], scalar2=None,
            op0=ALU.is_ge))
        sv_mask = cnt["sv"]
        wt("scalar", "sv", sv_mask)
        run("scalar", "sa", lambda s: s.activation(
            out=sigT, in_=score_cols, func=AF.Exp, scale=f32(-1.0)))
        run("scalar", "sa", lambda s: s.activation(
            out=sigT, in_=sigT, func=AF.Ln, bias=f32(1.0)))
        run("scalar", "sa", lambda s: s.activation(
            out=sigT, in_=sigT, func=AF.Exp, scale=f32(-1.0)))
        sa_sig = cnt["sa"]
        wt("vector", "sa", sa_sig)
        run("vector", "sv", lambda v: v.tensor_mul(
            out=wsel, in0=wsel, in1=sigT))
        sv_wsel = cnt["sv"]

        run("vector", "sv", lambda v: v.tensor_scalar(
            out=geD, in0=score_cols, scalar1=bis[:, 0:1], scalar2=0.0,
            op0=ALU.is_ge, op1=ALU.add, accum_out=bis[:, 4:5]))
        sv_m = cnt["sv"]
        wt("tensor", "sv", sv_m)
        wt("tensor", pb7c[0][0], pb7c[0][1])
        run("tensor", "sp", lambda t: t.matmul(
            pb7[:, 0:1], tri_f[:], bis[:, 4:5], start=True, stop=True))
        wt("scalar", "sp", cnt["sp"])
        run("scalar", "sa", lambda s: s.activation(
            out=bis[:, 5:6], in_=pb7[:, 0:1], func=AF.Identity))
        sa_rb = cnt["sa"]
        pb7c[0] = ("sa", cnt["sa"])
        run("vector", "sv", lambda v: v.memset(ramp16[:], 1.0))
        run("vector", "sv", lambda v: v.tensor_tensor_scan(
            out=csum16[:], data0=ramp16[:], data1=geD, initial=0.0,
            op0=ALU.mult, op1=ALU.add))
        run("vector", "sv", lambda v: v.tensor_sub(
            out=csum16[:], in0=csum16[:], in1=geD))
        wt("vector", "sa", sa_rb)
        run("vector", "sv", lambda v: v.tensor_single_scalar(
            out=csum16[:], in_=csum16[:], scalar=bis[:, 5:6], op=ALU.add))
        run("vector", "sv", lambda v: v.tensor_scalar(
            out=pairs_all[:, 0:16], in0=geD, scalar1=-4096.0, scalar2=4096.0,
            op0=ALU.mult, op1=ALU.add))
        run("vector", "sv", lambda v: v.tensor_mul(
            out=csum16[:], in0=csum16[:], in1=geD))
        run("vector", "sv", lambda v: v.tensor_add(
            out=csum16[:], in0=csum16[:], in1=pairs_all[:, 0:16]))
        run("vector", "sv", lambda v: v.tensor_copy(out=offsI[:], in_=csum16[:]))
        pairs_bf = pairs_all[:].bitcast(BF16)
        whi_b = csum16[:, 0:8].bitcast(BF16)
        wlo_b = csum16[:, 8:16].bitcast(BF16)
        run("vector", "sv", lambda v: v.tensor_single_scalar(
            out=whi_b, in_=wsel, scalar=0.5, op=ALU.mult))
        run("vector", "sv", lambda v: v.scalar_tensor_tensor(
            out=geD, in0=wsel, scalar=0.5, in1=whi_b,
            op0=ALU.mult, op1=ALU.subtract))
        run("vector", "sv", lambda v: v.tensor_copy(out=wlo_b, in_=geD))
        pv4 = pairs_bf.rearrange("p (i four) -> p i four", four=4)
        run("vector", "sv", lambda v: v.tensor_copy(
            out=pv4[:, :, 0:1], in_=whi_b.unsqueeze(2)))
        run("vector", "sv", lambda v: v.tensor_copy(
            out=pv4[:, :, 1:2], in_=wlo_b.unsqueeze(2)))
        run("vector", "sv", lambda v: v.tensor_copy(
            out=pv4[:, :, 2:3], in_=ibf[:].unsqueeze(2)))
        run("vector", "sv", lambda v: v.tensor_copy(
            out=pv4[:, :, 3:4], in_=pbf[:].unsqueeze(2)))
        sv_offs = cnt["sv"]
        sel_toks = []
        wt("gpsimd", "sv", sv_offs)
        pv_b = pairs_bf
        wt("sync", "sv", sv_offs)
        pair_w_toks = []
        for i in range(NT):
            pair_w_toks.append(dma(
                "sync", "so", xn_dram[i * 128:(i + 1) * 128, D:D + 4],
                pv_b[:, 4 * i:4 * i + 4]))
        tar_bf = tarena[:].bitcast(BF16)
        stg4 = [bcB[:].bitcast(BF16)[:, 0:1028], xnbf[:, 0:1028],
                tar_bf[:, 0:1028], tar_bf[:, 1028:2056]]
        sct_free = [None] * 4
        wt("sync", *sa_acc_tile[NT - 1])
        wt("sync", "sv", sv_att_end)
        sel_toks = []
        for i in range(NT):
            b = i % 4
            wtt("sync", [xn_w_toks[i], pair_w_toks[i]])
            if sct_free[b] is not None:
                wtt("sync", [sct_free[b]])
            tok_l = dma("sync", "sx", stg4[b], xn_dram[i * 128:(i + 1) * 128, :])
            wtt("gpsimd", [tok_l])
            tok_s = dma_indirect_out(
                "gpsimd", "sw", xsel_dram, offsI[:, i:i + 1], stg4[b], L // 2 - 1)
            sel_toks.append(tok_s)
            sct_free[b] = tok_s

        xsT_view = warena[:, 0:8192].rearrange("p (fi s) -> p fi s", s=1024)
        stage_free = [("sp", 0)] * 4
        sv_wtsl = 0
        wtt("sync", sel_toks)
        wt("tensor", "sv", sv_att_end)
        trc = {7: pb7c[0], 6: obank_free[3]}
        trbf = {7: pb7_bf, 6: pb6_bf}
        for st in range(8):
            b = st % 4
            stg = stg4[b]
            bk = 7 if st % 2 == 0 else 6
            wt("sync", stage_free[b][0], stage_free[b][1])
            if sv_wtsl:
                wt("sync", "sv", sv_wtsl)
            if st < 2:
                wt("sync", "sp", sp_rkv_end)
            tok_s = dma("sync", "sx", stg, xsel_dram[st * 128:(st + 1) * 128, :])
            wtt("tensor", [tok_s])
            wt("tensor", trc[bk][0], trc[bk][1])
            for fi in range(8):
                run("tensor", "sp", lambda t, fi=fi, stg=stg, bk=bk: t.matmul(
                    trbf[bk][:, fi * 128:(fi + 1) * 128],
                    stg[:, fi * 128:(fi + 1) * 128],
                    ident_b[:], start=(fi == 0), stop=(fi == 7),
                    is_transpose=True))
            sp_st = cnt["sp"]
            stage_free[b] = ("sp", sp_st)
            wt("scalar", "sp", sp_st)
            run("scalar", "sa", lambda s, st=st, bk=bk: s.activation(
                out=xsT_view[:, :, st * 128:(st + 1) * 128],
                in_=trbf[bk].rearrange("p (fi t) -> p fi t", t=128),
                func=AF.Identity))
            trc[bk] = ("sa", cnt["sa"])
            wtt("vector", [tok_s])
            run("vector", "sv", lambda v, st=st, stg=stg: v.tensor_add(
                out=wt_sl[:, 2 * st:2 * st + 1],
                in0=stg[:, 1024:1025], in1=stg[:, 1025:1026]))
            run("vector", "sv", lambda v, st=st, stg=stg: v.scalar_tensor_tensor(
                out=wt_sl[:, 2 * st + 1:2 * st + 2],
                in0=stg[:, 1026:1027], scalar=128.0,
                in1=stg[:, 1027:1028], op0=ALU.mult, op1=ALU.add))
            sv_wtsl = cnt["sv"]
        sa_xsel = cnt["sa"]
        sp_xsel = cnt["sp"]
        run("vector", "sv", lambda v: v.tensor_copy(
            out=toffI[:], in_=wt_sl[:].rearrange(
                "p (i two) -> p i two", two=2)[:, :, 1:2].squeeze(2)))
        sv_toff = cnt["sv"]

        wt("gpsimd", "sp", sp_xsel)
        sw_b2 = [dma("gpsimd", "sw", prow1[:, :], b2_hbm.unsqueeze(0))]
        wtt("tensor", sw_b2)
        wt("tensor", "sa", sa_att_end)
        run("tensor", "sp", lambda t: t.matmul(
            pb3[:, 0:512], ones_f[:, :], prow1[:, 0:512], start=True, stop=True))
        run("tensor", "sp", lambda t: t.matmul(
            pb4[:, 0:512], ones_f[:, :], prow1[:, 512:1024], start=True, stop=True))
        sp_b2B = cnt["sp"]
        wt("scalar", "sp", sp_b2B)
        wt("scalar", "sp", sp_xsel)
        run("scalar", "sa", lambda s: s.activation(
            out=bcB[:, 0:512], in_=pb3[:, 0:512], func=AF.Identity))
        run("scalar", "sa", lambda s: s.activation(
            out=bcB[:, 512:1024], in_=pb4[:, 0:512], func=AF.Identity))
        sa_b2B = cnt["sa"]

        wt("vector", "sa", sa_b2B)
        for i in range(NT):
            run("vector", "sv", lambda v, i=i: v.scalar_tensor_tensor(
                out=acc[:, i * D:(i + 1) * D], in0=bcB[:],
                scalar=wsel[:, i:i + 1], in1=acc[:, i * D:(i + 1) * D],
                op0=ALU.mult, op1=ALU.add))
        sv_b2 = cnt["sv"]

        wt("scalar", "sv", sv_b2)
        prefill_toks = []
        for i in range(NT):
            prefill_toks.append(dma(
                "scalar", "sz", upd_dram[i * 128:(i + 1) * 128, :],
                acc[:, i * D:(i + 1) * D]))

        TFS = 512
        NTFS = 2

        def load_w(q, which, gate_sp):
            if gate_sp:
                wt("sync", "sp", gate_sp)
            wtt("sync", cast_w12[0:1] if which == 1 else cast_w12[1:2])
            base = W1B[q] if which == 1 else W2B[q]
            srcf = w1q_srcb if which == 1 else w2q_srcb
            return dma("sync", "sx", warena[:, base:base + 8192].rearrange(
                "p (fi F) -> p fi F", F=1024), srcf(q))

        sw_q = {0: [load_w(0, 1, sp_rkv_end), load_w(0, 2, sp_rkv_end)]}
        sw_q[1] = [load_w(1, 1, sp_wo_end)]
        sp_qdone = {}
        sp_h_done = {}
        hbank_free = [("sv", 0), ("sv", 0),
                      ("sa", sa_b2B), ("sa", sa_b2B)]
        ubank_free = [("sa", sa_b2B)] * 4
        gslot_free = [("sv", 0), ("sv", 0)]
        hslice_free = [("sv", 0)] * 8
        hbf_gate = [("sv", 0), ("sv", 0)]
        updsb_free = [("sv", 0), ("sv", 0)]
        upd_toks = []
        upd_prev = {}

        hbufS = tarena[:, 0:4096]
        hbf16S = tarena[:, 5120:7168].bitcast(BF16)
        HB = [hbf16S, xnT[:, 0:4096]]
        upd_sb2 = [tarena[:, 7168:8192], xnbf[:].bitcast(F32)]

        def emit_h_gelu(q, tc):
            wtt("tensor", sw_q[q])
            if q == 0 and tc == 0:
                wt("tensor", "sa", sa_xsel)
            for ffi in range(8):
                bk = ffi % 4
                wt("tensor", hbank_free[bk][0], hbank_free[bk][1])
                for fi in range(8):
                    run("tensor", "sp", lambda t, q=q, tc=tc, ffi=ffi, fi=fi, bk=bk:
                        t.matmul(pb[bk][:, 0:TFS],
                                 warena[:, W1B[q] + fi * 1024 + ffi * 128:
                                        W1B[q] + fi * 1024 + (ffi + 1) * 128],
                                 warena[:, fi * 1024 + tc * TFS:
                                        fi * 1024 + (tc + 1) * TFS],
                                 start=(fi == 0), stop=(fi == 7)))
                sp_h = cnt["sp"]
                gs = ffi % 2
                gv = gbufh[:, gs * 512:(gs + 1) * 512]
                hv = hbufS[:, ffi * TFS:(ffi + 1) * TFS]
                wt("scalar", "sp", sp_h)
                wt("scalar", hslice_free[ffi][0], hslice_free[ffi][1])
                if q == 0 and tc == 0 and ffi == 0:
                    wt("scalar", "sv", sv_att_end)
                    wt("scalar", "sg", sg_att_end)
                    wt("scalar", "sp", sp_xsel)
                run("scalar", "sa", lambda s, ffi=ffi, q=q, bk=bk, hv=hv: s.activation(
                    out=hv, in_=pb[bk][:, 0:TFS], func=AF.Identity,
                    bias=pcols[:, 40 + q * 8 + ffi:41 + q * 8 + ffi]))
                hbank_free[bk] = ("sa", cnt["sa"])
                wt("scalar", gslot_free[gs][0], gslot_free[gs][1])
                run("scalar", "sa", lambda s, hv=hv, gv=gv: s.activation(
                    out=gv, in_=hv, func=AF.Square, scale=GEL_A))
                sa_sq = cnt["sa"]
                wt("vector", "sa", sa_sq)
                run("vector", "sv", lambda v, hv=hv, gv=gv: v.scalar_tensor_tensor(
                    out=gv, in0=gv, scalar=1.0, in1=hv,
                    op0=ALU.add, op1=ALU.mult))
                sv_z = cnt["sv"]
                wt("scalar", "sv", sv_z)
                run("scalar", "sa", lambda s, gv=gv: s.activation(
                    out=gv, in_=gv, func=AF.Tanh, scale=GEL_B))
                sa_t = cnt["sa"]
                wt("vector", "sa", sa_t)
                wt("vector", hbf_gate[tc][0], hbf_gate[tc][1])
                if q == 0 and tc == 1 and ffi == 0:
                    wt("vector", "sp", sp_rkv_end)
                run("vector", "sv",
                    lambda v, hv=hv, gv=gv, ffi=ffi, tc=tc: v.scalar_tensor_tensor(
                        out=HB[tc][:, ffi * TFS:(ffi + 1) * TFS], in0=gv,
                        scalar=1.0, in1=hv, op0=ALU.add, op1=ALU.mult))
                sv_tt = cnt["sv"]
                gslot_free[gs] = ("sv", sv_tt)
                hslice_free[ffi] = ("sv", sv_tt)
            return cnt["sv"]

        def emit_upd(q, tc, sv_hbf):
            wt("tensor", "sv", sv_hbf)
            for sub in range(4):
                st = tc * 4 + sub
                for dh in range(2):
                    bank = 4 + (sub % 2) * 2 + dh
                    wt("tensor", ubank_free[bank - 4][0], ubank_free[bank - 4][1])
                    for ffi in range(8):
                        run("tensor", "sp",
                            lambda t, q=q, tc=tc, dh=dh, sub=sub, ffi=ffi,
                            bank=bank:
                            t.matmul(pb[bank][:, 0:512],
                                     HB[tc][:, ffi * TFS + sub * 128:
                                            ffi * TFS + (sub + 1) * 128],
                                     warena[:, W2B[q] + ffi * 1024 + dh * 512:
                                            W2B[q] + ffi * 1024 + (dh + 1) * 512],
                                     start=(ffi == 0), stop=(ffi == 7)))
                    sp_u = cnt["sp"]
                    wt("vector", "sp", sp_u)
                    if q == 0 and tc == 0 and sub == 0 and dh == 0:
                        wt("vector", "sv", sv_toff)
                        wt("vector", "sp", sp_wo_end)
                    if q == 0 and tc == 0 and sub == 1 and dh == 0:
                        wt("vector", "sp", sp_xsel)
                    ub2 = st % 2
                    wt("vector", updsb_free[ub2][0], updsb_free[ub2][1])
                    run("vector", "sv",
                        lambda v, st=st, dh=dh, bank=bank, ub2=ub2:
                        v.tensor_single_scalar(
                            out=upd_sb2[ub2][:, dh * 512:(dh + 1) * 512],
                            in_=pb[bank][:, 0:512],
                            scalar=wt_sl[:, 2 * st:2 * st + 1], op=ALU.mult))
                    ubank_free[bank - 4] = ("sv", cnt["sv"])
                sv_st = cnt["sv"]
                wt("gpsimd", "sv", sv_st)
                if q == 0:
                    wtt("gpsimd", prefill_toks)
                if st in upd_prev:
                    wtt("gpsimd", [upd_prev[st]])
                tok_u = dma_indirect_out(
                    "gpsimd", "sw", upd_dram, toffI[:, st:st + 1],
                    upd_sb2[st % 2], L - 1, cce=ALU.add)
                upd_toks.append(tok_u)
                upd_prev[st] = tok_u
                updsb_free[st % 2] = tok_u
            hbf_gate[tc] = ("sp", cnt["sp"])

        for q in range(NQ):
            sv_hbf = {}
            for tc in range(NTFS):
                sv_hbf[tc] = emit_h_gelu(q, tc)
            sp_h_done[q] = cnt["sp"]
            for tc in range(NTFS):
                emit_upd(q, tc, sv_hbf[tc])
            sp_qdone[q] = cnt["sp"]
            if q == 0:
                sw_q[1].append(load_w(1, 2, sp_h_done[0]))
            elif q == 1:
                sw_q[2] = [load_w(2, 1, sp_qdone[0]),
                           load_w(2, 2, sp_h_done[1])]
            elif q == 2:
                sw_q[3] = [load_w(3, 1, sp_qdone[1]),
                           load_w(3, 2, sp_h_done[2])]
        sv_ffn = cnt["sv"]
        sa_ffn = cnt["sa"]
        sp_ffn = cnt["sp"]

        so_toks = []
        wtt("scalar", upd_toks)
        tins = [dma("scalar", "sz", acc[:, i * D:(i + 1) * D],
                    upd_dram[i * 128:(i + 1) * 128, :]) for i in range(NT)]
        for i in range(NT):
            wtt("sync", [tins[i]])
            so_toks.append(dma("sync", "so", y_hbm[i * 128:(i + 1) * 128, :],
                               acc[:, i * D:(i + 1) * D]))
        ckpt("ffn")
        if debug:
            wt("sync", "sv", cnt["sv"])
            wt("sync", "sa", cnt["sa"])
            so_toks.append(dma("sync", "so", dbg["d_scores"], score_cols))
            so_toks.append(dma("sync", "so", dbg["d_thr"], bis[:, 0:4]))
            so_toks.append(dma("sync", "so", dbg["d_wsel"], wsel))
            so_toks.append(dma("sync", "so", dbg["d_acc"], acc[:]))
        wtt("sync", so_toks)

        @block.sync
        def _(e):
            for fn in plan["sync"]:
                fn(e)

        @block.gpsimd
        def _(e):
            for fn in plan["gpsimd"]:
                fn(e)

        @block.tensor
        def _(e):
            for fn in plan["tensor"]:
                fn(e)

        @block.scalar
        def _(e):
            for fn in plan["scalar"]:
                fn(e)

        @block.vector
        def _(e):
            for fn in plan["vector"]:
                fn(e)

    return nc


_NC_CACHE = {}


def kernel(**inputs):
    n = 8
    key = "prog"
    if key not in _NC_CACHE:
        _NC_CACHE[key] = build_program(debug=False)
    nc = _NC_CACHE[key]
    x = np.ascontiguousarray(inputs["L0_attn_inputs"], dtype=np.float32)
    common = {
        "W_rkv": np.ascontiguousarray(inputs["L0_attn_W_rkv"], np.float32),
        "b_rkv": np.ascontiguousarray(inputs["L0_attn_b_rkv"], np.float32),
        "W_o": np.ascontiguousarray(inputs["L0_attn_W_o"], np.float32),
        "b_o": np.ascontiguousarray(inputs["L0_attn_b_o"], np.float32),
        "nu_log": np.ascontiguousarray(inputs["L0_attn_nu_log"], np.float32),
        "gamma_log": np.ascontiguousarray(inputs["L0_attn_gamma_log"], np.float32),
        "w_router": np.ascontiguousarray(inputs["L0_ffn_w_router"], np.float32),
        "W1": np.ascontiguousarray(inputs["L0_ffn_W1"], np.float32),
        "b1": np.ascontiguousarray(inputs["L0_ffn_b1"], np.float32),
        "W2": np.ascontiguousarray(inputs["L0_ffn_W2"], np.float32),
        "b2": np.ascontiguousarray(inputs["L0_ffn_b2"], np.float32),
    }
    in_maps = [dict(common, x=np.ascontiguousarray(x[b])) for b in range(n)]
    res = run_bass_kernel_spmd(nc, in_maps, core_ids=list(range(n)))
    out = np.stack([r["y"] for r in res.results], axis=0)
    return out
